# version 16
# baseline (speedup 1.0000x reference)
import numpy as np
import os
import ml_dtypes
import concourse.bass as bass
import concourse.mybir as mybir
from concourse.bass_utils import run_bass_kernel_spmd

F32 = mybir.dt.float32
BF16 = mybir.dt.bfloat16
AF = mybir.ActivationFunctionType
ALU = mybir.AluOpType
AX = mybir.AxisListType
BFD = ml_dtypes.bfloat16

NCORES = 8
B_, N, D = 4, 4096, 1024
NH, DH, NLM, ITERS, KC = 8, 64, 256, 6, 13
TOK = 2048
E, FF = 8, 2048
EPS = 1e-5
QS = 0.125


class Prog:
    ENGS = ("sync", "scalar", "vector", "gpsimd", "tensor")
    DMA_SEMS = ("cst", "xA0", "xA1", "ast", "xm0", "xm1", "fin", "tap",
                "spr", "qw0", "qw1", "gw", "gd", "cc")

    def __init__(self):
        self.ops = {e: [] for e in self.ENGS}
        self.counts = {s: 0 for s in self.ENGS + self.DMA_SEMS}

    def add(self, eng, fn, wait=(), inc_by=0, sem=None):
        tok = None
        sem = sem or eng
        if inc_by:
            self.counts[sem] += inc_by
            tok = (sem, self.counts[sem])
        self.ops[eng].append((fn, [w for w in wait if w is not None], inc_by, sem, eng))
        return tok

    def all_toks(self):
        return [(s, c) for s, c in self.counts.items() if c > 0]

    def emit(self, block, sems):
        def runner(name):
            def _run(engine):
                waited = {}
                for fn, waits, inc_by, sem, eng in self.ops[name]:
                    for (s, v) in waits:
                        if waited.get(s, -1) >= v:
                            continue
                        engine.wait_ge(sems[s], v)
                        waited[s] = v
                    inst = fn(engine)
                    if inc_by:
                        inst.then_inc(sems[sem], inc_by)
            return _run
        block.sync(runner("sync"))
        block.scalar(runner("scalar"))
        block.vector(runner("vector"))
        block.gpsimd(runner("gpsimd"))
        block.tensor(runner("tensor"))


class Bank:
    def __init__(self, ap):
        self.ap = ap
        self.prod = None
        self.cons = None


def build_nc(debug_taps=False):
    nc = bass.Bass("TRN2", target_bir_lowering=False)

    d_x = nc.dram_tensor("xb", [N, D], F32, kind="ExternalInput").ap()
    d_xo = nc.dram_tensor("xown", [TOK, D], F32, kind="ExternalInput").ap()
    d_wqkv = nc.dram_tensor("wqkv", [4, 128, 8, 384], F32, kind="ExternalInput").ap()
    d_wfc = nc.dram_tensor("wfc", [D, D], F32, kind="ExternalInput").ap()
    d_bfc = nc.dram_tensor("bfc", [1, D], F32, kind="ExternalInput").ap()
    d_wconv = nc.dram_tensor("wconv", [NH, KC], F32, kind="ExternalInput").ap()
    d_rwT = nc.dram_tensor("rwT", [D, E], F32, kind="ExternalInput").ap()
    d_wgT = nc.dram_tensor("wgT", [E, D, FF], F32, kind="ExternalInput").ap()
    d_wuT = nc.dram_tensor("wuT", [E, D, FF], F32, kind="ExternalInput").ap()
    d_wdT = nc.dram_tensor("wdT", [E, FF, D], F32, kind="ExternalInput").ap()
    d_eye = nc.dram_tensor("eye_bf", [128, 128], BF16, kind="ExternalInput").ap()
    d_eye7 = nc.dram_tensor("eye7_bf", [128, 128], BF16, kind="ExternalInput").ap()
    d_eye15 = nc.dram_tensor("eye15_bf", [128, 128], BF16, kind="ExternalInput").ap()
    d_eye325 = nc.dram_tensor("eye325_bf", [128, 128], BF16, kind="ExternalInput").ap()
    d_onesc = nc.dram_tensor("ones_col_bf", [128, 1], BF16, kind="ExternalInput").ap()
    d_onesr = nc.dram_tensor("ones_row_bf", [1, 128], BF16, kind="ExternalInput").ap()
    d_onesrf = nc.dram_tensor("ones_row_f", [1, 128], F32, kind="ExternalInput").ap()
    d_pairsel = nc.dram_tensor("pairsel", [2, 128], F32, kind="ExternalInput").ap()

    d_y = nc.dram_tensor("y", [TOK, D], F32, kind="ExternalOutput").ap()

    i_attnT = nc.dram_tensor("i_attnT", [NH * DH, N], BF16, kind="Internal")
    i_pairT = nc.dram_tensor("i_pairT", [2 * NH * DH, N], BF16, kind="Internal")
    i_xmid = nc.dram_tensor("i_xmid", [TOK, D], F32, kind="Internal").ap()

    taps = {}
    if debug_taps:
        taps["t_xnT"] = nc.dram_tensor("t_xnT", [128, 8 * N], BF16, kind="ExternalOutput").ap()
        taps["t_attnT"] = nc.dram_tensor("t_attnT", [NH * DH, N], BF16, kind="ExternalOutput").ap()
        taps["t_xmid"] = nc.dram_tensor("t_xmid", [TOK, D], F32, kind="ExternalOutput").ap()
        taps["t_gates"] = nc.dram_tensor("t_gates", [128, 16 * E], F32, kind="ExternalOutput").ap()
        taps["t_xnlT"] = nc.dram_tensor("t_xnlT", [128, 8 * NLM], BF16, kind="ExternalOutput").ap()
        taps["t_lgt"] = nc.dram_tensor("t_lgt", [128, 16 * E], F32, kind="ExternalOutput").ap()
        taps["t_a2"] = nc.dram_tensor("t_a2", [128, 512], BF16, kind="ExternalOutput").ap()
        taps["t_z"] = nc.dram_tensor("t_z", [128, 512], BF16, kind="ExternalOutput").ap()
        taps["t_o3"] = nc.dram_tensor("t_o3", [128, 128], BF16, kind="ExternalOutput").ap()
        taps["t_rr"] = nc.dram_tensor("t_rr", [128, 128], BF16, kind="ExternalOutput").ap()
        taps["t_e3T"] = nc.dram_tensor("t_e3T", [128, 1024], BF16, kind="ExternalOutput").ap()
        taps["t_v2"] = nc.dram_tensor("t_v2", [128, 2 * TOK], BF16, kind="ExternalOutput").ap()
        taps["t_s3"] = nc.dram_tensor("t_s3", [128, 24], F32, kind="ExternalOutput").ap()

    from contextlib import ExitStack
    ctx = ExitStack()
    with ctx:
        ORIGIN = 19456
        TOP = 229344
        DSZ = {BF16: 2, F32: 4}
        placed = []
        placed_meta = []
        placed_lo = {}

        def place(name, shape, dtype, p0, p1, at=None, at_off=0):
            nbytes = int(np.prod(shape[1:])) * DSZ[dtype]
            nbytes = (nbytes + 31) // 32 * 32
            if at is not None:
                lo = placed_lo[at] + at_off
                placed.append((lo, lo + nbytes, p0, p1))
            else:
                conf = sorted([(lo, hi) for (lo, hi, q0, q1) in placed
                               if not (q1 < p0 or p1 < q0)])
                lo = ORIGIN
                for (clo, chi) in conf:
                    if lo + nbytes <= clo:
                        break
                    lo = max(lo, chi)
                if lo + nbytes > TOP:
                    for nm, l_ in sorted(placed_lo.items(), key=lambda kv: kv[1]):
                        print(f"  {nm:12s} lo={l_}")
                    raise AssertionError(f"sbuf overflow at {name}: {lo}+{nbytes}")
                placed.append((lo, lo + nbytes, p0, p1))
            placed_meta.append((lo, lo + nbytes, p0, p1, name, at))
            placed_lo[name] = lo
            h = nc.alloc_sbuf_tensor_at(name, list(shape), dtype, offset=lo)
            return h.ap()

        def place3(name, outer, inner, dtype, p0, p1, at=None, at_off=0):
            return place(name, [128, outer * inner], dtype, p0, p1, at=at,
                         at_off=at_off).rearrange("p (a b) -> p a b", b=inner)

        cb = place("cb", [128, 128], BF16, 0, 3)
        cbA = place("cbA", [128, 384], BF16, 0, 1)
        eye, eye7 = cb[:, 0:128], cbA[:, 0:128]
        eye15, eye325 = cbA[:, 128:256], cbA[:, 256:384]
        ones_col = place("ones_col", [128, 1], BF16, 0, 3)
        ones_row = place("ones_row", [1, 128], BF16, 0, 3)
        ones_rowf = place("ones_rowf", [1, 128], F32, 0, 3)
        smc = place("smc", [128, 24], F32, 0, 2)
        smcA = place("smcA", [128, 96], F32, 0, 1)
        lncol = [smc[:, 0:8], smc[:, 8:16]]
        m3c = [smcA[:, 0:16], smcA[:, 16:32]]
        hc = [smcA[:, 32:56], smcA[:, 56:80]]
        s1c = [smcA[:, 80:84], smcA[:, 84:88], smcA[:, 88:92], smcA[:, 92:96]]
        epsc = smc[:, 16:17]
        lgt_b = place("lgt", [128, 16], F32, 2, 2)
        lgt2 = [lgt_b[:, 0:8], lgt_b[:, 8:16]]
        sx = place("sx", [128, 192], F32, 2, 2)
        _sxn = ["ind1", "absS", "f1", "d", "t", "u1", "ebuf", "eu", "ms", "ind2", "tmp", "d2"]
        sxv2 = [{k: sx[:, (sl * 12 + i) * 8:(sl * 12 + i + 1) * 8]
                 for i, k in enumerate(_sxn)} for sl in range(2)]
        onebuf = place("onebuf", [1, 16], F32, 1, 1)
        rws = place("rws", [128, 64], F32, 0, 1)
        rw_hi = place3("rw_hi", 8, E, BF16, 0, 2)
        rw_lo = place3("rw_lo", 8, E, BF16, 0, 2)
        wconv_t = place("wconv_t", [2, 4 * KC], F32, 0, 1)
        pairsel_t = place("pairsel_t", [2, 128], F32, 0, 1)
        convS = place("convS", [128, KC], F32, 0, 3)
        gcol = place3("gcol", 16, E, F32, 2, 3)

        xnT = place3("xnT", 8, N, BF16, 1, 1)
        wgt = place3("wgt", 8, FF, BF16, 2, 3, at="xnT")
        wut = place3("wut", 8, FF, BF16, 2, 3, at="xnT", at_off=8 * FF * 2)
        pair_own = place3("pair_own", 8, TOK, BF16, 2, 2)
        xn2T = place3("xn2T", 8, TOK, BF16, 2, 3)
        wfc_bf = place3("wfc_bf", 8, 1024, BF16, 2, 2)
        bfc_t = place("bfc_t", [128, 1024], F32, 2, 2)
        bfc_row = place("bfc_row", [1, 1024], F32, 2, 2)
        xtB = [place(f"xtB{i}", [128, 1024], F32, 2, 2) for i in range(4)]
        smcB = place("smcB", [128, 16], F32, 2, 2)
        xloT_st = place3("xloT_st", 8, 128, BF16, 2, 2)

        wdt = place3("wdt", 16, 1024, BF16, 3, 3)
        hT = place3("hT", 16, 1024, BF16, 3, 3)
        out_acc = place3("out_acc", 8, 1024, F32, 3, 3)
        silu_st = [place("silu0", [128, 512], F32, 3, 3),
                   place("silu1", [128, 512], F32, 3, 3)]
        xtF = place("xtF", [128, 1024], F32, 3, 3, at="bfc_row")
        sq_f32 = place("sq_f32", [128, 1024], F32, 1, 2)
        taps_lgt_sb = place3("taps_lgt", 16, E, F32, 2, 2) if debug_taps else None

        wq_pp = [place3("wq0", 8, 384, BF16, 1, 1), place3("wq1", 8, 384, BF16, 1, 1)]
        qT2 = place("qT2", [128, N], BF16, 1, 1)
        kT2 = place("kT2", [128, N], BF16, 1, 1)
        vv2v = place("vv2", [128, 2 * 32 * 65], BF16, 1, 1).rearrange(
            "p (j m c) -> p j m c", m=32, c=65)
        sim3 = place3("sim3", 2, N, BF16, 1, 1)
        e3 = place3("e3", 2, N, BF16, 1, 1)
        e3T = place3("e3T", 32, 256, BF16, 1, 1, at="sim3")
        conv_buf = place("conv_buf", [128, 4108], BF16, 1, 1)

        res_pp = [place("res0", [128, TOK], BF16, 1, 2),
                  place("res1", [128, TOK], BF16, 1, 2)]
        xn_st = [place(f"xn_st{i}", [128, 1024], BF16, 1, 2) for i in range(2)]
        mp2 = [{k: place3(f"mp{i}_" + k, 2, 256, BF16, 1, 1)
                for k in ["a", "aT", "zA", "zB", "azTn", "az", "u", "w2", "wT"]}
               for i in range(2)]
        rrb2 = [place3(f"rrb{i}", 2, 64, BF16, 1, 1) for i in range(2)]
        e1T2 = [place3("e1T0", 4, 512, BF16, 1, 1, at="e3"),
                place3("e1T1", 4, 512, BF16, 1, 1, at="e3", at_off=4096)]
        usb_sh = place("usb_sh", [128, 512], BF16, 1, 1, at="e3", at_off=8192)
        ssb_sh = place("ssb_sh", [128, 512], BF16, 1, 1, at="e3", at_off=9216)
        onesp = place("onesp", [128, 64], BF16, 1, 1)
        out3b2 = [place3(f"out3b{i}", 2, 64, BF16, 1, 1) for i in range(2)]
        qlT2 = place("qlT2", [128, 256], BF16, 1, 1)
        klT2 = place("klT2", [128, 256], BF16, 1, 1)
        xnlT = place3("xnlT", 8, 256, BF16, 1, 1)
        sbc = place("sbc", [128, 128], BF16, 1, 1)
        cdiag = place3("cdiag", KC, 128, BF16, 1, 1)
        xtA = [place("xtA0", [128, 1024], F32, 1, 1),
               place("xtA1", [128, 1024], F32, 1, 1),
               place("xtA2", [128, 1024], F32, 1, 1, at="conv_buf", at_off=32)]
        lnc2 = place("lnc2", [128, 8], F32, 1, 1)
        xnl_f = place("xnl_f", [128, 256], F32, 1, 1)

        peak = max(hi for (lo, hi, _, _) in placed)
        assert peak <= TOP, peak
        if os.environ.get("BASS_LAYOUT"):
            for lo, hi, p0, p1, nm, at in sorted(placed_meta):
                print(f"  {lo:7d}..{hi:7d} ({hi-lo:6d}) p{p0}{p1} {nm}"
                      + (f" at={at}" if at else ""))
            print(f"  peak={peak} TOP={TOP}")

        pA = Bank(ctx.enter_context(nc.psum_tensor("pA", [128, 512], F32)).ap())
        pB = Bank(ctx.enter_context(nc.psum_tensor("pB", [128, 512], F32)).ap())
        pC = Bank(ctx.enter_context(nc.psum_tensor("pC", [128, 512], F32)).ap())
        pD = Bank(ctx.enter_context(nc.psum_tensor("pD", [128, 512], F32)).ap())
        pS = Bank(ctx.enter_context(nc.psum_tensor("pS", [128, 512], F32)).ap())
        pH = Bank(ctx.enter_context(nc.psum_tensor("pH", [128, 512], F32)).ap())
        pT1 = Bank(ctx.enter_context(nc.psum_tensor("pT1", [128, 1024], BF16)).ap())
        pT2 = Bank(ctx.enter_context(nc.psum_tensor("pT2", [128, 1024], BF16)).ap())

        sems = {s: ctx.enter_context(nc.semaphore("sem_" + s))
                for s in Prog.ENGS + Prog.DMA_SEMS}
        P = Prog()

        def dma(eng, sem, out, in_, wait=()):
            return P.add(eng, lambda E_, o=out, i=in_: E_.dma_start(out=o, in_=i),
                         wait=wait, inc_by=16, sem=sem)

        def mm(bank, lhsT, rhs_, start, stop, wait=(), out=None, inc=0):
            o = bank.ap if out is None else out
            w = list(wait)
            if start and bank.cons is not None:
                w.append(bank.cons)
            t = P.add("tensor",
                      lambda E_, o=o, l=lhsT, r=rhs_, s1=start, s2=stop:
                      E_.matmul(o, l, r, start=s1, stop=s2),
                      wait=w, inc_by=inc)
            if inc:
                bank.prod = t
            return t

        def trans(bank, in_, out=None, wait=(), inc=0):
            o = bank.ap if out is None else out
            w = list(wait)
            if bank.cons is not None:
                w.append(bank.cons)
            t = P.add("tensor", lambda E_, o=o, i=in_: E_.transpose(o, i, eye),
                      wait=w, inc_by=inc)
            if inc:
                bank.prod = t
            return t

        def ev(eng, fn, bank=None, wait=(), inc_by=1):
            w = list(wait)
            if bank is not None and bank.prod is not None:
                w.append(bank.prod)
            t = P.add(eng, fn, wait=w, inc_by=inc_by)
            if bank is not None:
                bank.cons = t
            return t

        dma("sync", "cst", eye, d_eye)
        dma("sync", "cst", eye7, d_eye7)
        dma("sync", "cst", eye15, d_eye15)
        dma("sync", "cst", eye325, d_eye325)
        dma("sync", "cst", ones_col[:], d_onesc)
        dma("sync", "cst", ones_row[:], d_onesr)
        dma("sync", "cst", ones_rowf[:], d_onesrf)
        dma("sync", "cst", pairsel_t[:], d_pairsel)
        t_wc = None
        for p in range(4):
            dma("sync", "cst", wconv_t[:, p * KC:(p + 1) * KC],
                d_wconv[2 * p:2 * p + 2, :])
        t_eps = P.add("vector", lambda E_: E_.memset(epsc, EPS), inc_by=1)
        t_cbz = P.add("vector", lambda E_: E_.memset(conv_buf[:, 0:6], 0.0), inc_by=1)
        t_cbz2 = P.add("vector", lambda E_: E_.memset(conv_buf[:, 4102:4108], 0.0),
                       inc_by=1)
        P.add("vector", lambda E_: E_.memset(onesp[0:1, :], 1.0), inc_by=1)
        tok_vones = None
        for jj_ in range(2):
            tok_vones = P.add("vector", lambda E_, jj_=jj_: E_.memset(
                vv2v[:, jj_, :, 64:65], 1.0), inc_by=1)
        tok_rrones = P.add("vector", lambda E_: E_.memset(
            onesp[64:65, :], 1.0), inc_by=1)
        pid_cell = {}
        def _pid_init(E_):
            pid_cell["pid"] = E_.partition_id()
            return E_.nop()
        P.add("scalar", _pid_init, inc_by=0)
        rwsv = rws.rearrange("p (a b) -> p a b", b=E)
        dma("sync", "cst", rwsv, d_rwT.rearrange("(c p) e -> p c e", p=128))
        tok_consts0 = ("cst", P.counts["cst"])
        t_rwh = P.add("vector", lambda E_: E_.tensor_copy(
            rw_hi[:], rwsv), wait=[tok_consts0], inc_by=1)
        t_rwl = P.add("vector", lambda E_: E_.tensor_tensor(
            rw_lo[:], rwsv, rw_hi[:], op=ALU.subtract), wait=[t_rwh], inc_by=1)
        tok_consts = ("cst", P.counts["cst"])

        sq_scr = sq_f32
        t_norm = [None] * 32
        t_sq = [None] * 32
        t_trev = [None] * 32
        lncolA = [lncol[0], lncol[1], lnc2[:, 0:8]]
        for t in range(32):
            xt = xtA[t % 3]
            lc = lncolA[t % 3]
            w = [tok_consts] if t == 0 else []
            if t >= 3:
                w += [t_norm[t - 3], t_sq[t - 3]]
            t_xd0 = dma("sync", "xA0", xt[:, 0:512],
                        d_x[128 * t:128 * (t + 1), 0:512], wait=w)
            t_xd = dma("gpsimd", "xA1", xt[:, 512:1024],
                       d_x[128 * t:128 * (t + 1), 512:1024], wait=w)
            t_sum = P.add("vector", lambda E_, xt=xt, lc=lc: E_.reduce_sum(
                lc[:, 0:1], xt, axis=AX.X), wait=[t_xd0, t_xd], inc_by=1)
            t_sq[t] = P.add("scalar", lambda E_, xt=xt, lc=lc: E_.activation(
                sq_scr, xt, AF.Square, accum_out=lc[:, 1:2]),
                wait=[t_xd0, t_xd], inc_by=1)
            t_q = P.add("vector", lambda E_, lc=lc: E_.scalar_tensor_tensor(
                lc[:, 2:3], lc[:, 0:1], -1.0 / D, lc[:, 0:1], op0=ALU.mult, op1=ALU.mult),
                wait=[t_sum], inc_by=1)
            t_var = P.add("vector", lambda E_, lc=lc: E_.tensor_tensor(
                lc[:, 4:5], lc[:, 1:2], lc[:, 2:3], op=ALU.add),
                wait=[t_q, t_sq[t]], inc_by=1)
            t_ln = P.add("scalar", lambda E_, lc=lc: E_.activation(
                lc[:, 5:6], lc[:, 4:5], AF.Ln, bias=epsc, scale=1.0 / D),
                wait=[t_var, t_eps], inc_by=1)
            t_rs = P.add("scalar", lambda E_, lc=lc: E_.activation(
                lc[:, 6:7], lc[:, 5:6], AF.Exp, scale=-0.5), wait=[t_ln], inc_by=1)
            t_nb = P.add("vector", lambda E_, lc=lc: E_.scalar_tensor_tensor(
                lc[:, 7:8], lc[:, 0:1], -1.0 / D, lc[:, 6:7], op0=ALU.mult, op1=ALU.mult),
                wait=[t_rs], inc_by=1)
            xs = xn_st[t % 2]
            wnorm = [t_nb] + ([t_trev[t - 2]] if t >= 2 else [])
            t_norm[t] = P.add("vector", lambda E_, xt=xt, lc=lc, xs=xs: E_.tensor_scalar(
                xs, xt, lc[:, 6:7], lc[:, 7:8], op0=ALU.mult, op1=ALU.add),
                wait=wnorm, inc_by=1)
            bankT = pT1 if t % 2 == 0 else pT2
            for kc in range(8):
                trans(bankT, xs[0:128, kc * 128:(kc + 1) * 128],
                      out=bankT.ap[:, kc * 128:(kc + 1) * 128],
                      wait=[t_norm[t]] if kc == 0 else (), inc=(kc == 7))
            if t % 2 == 0:
                t_trev[t] = ev("scalar", lambda E_, t=t, bankT=bankT: E_.copy(
                    xnT[:, :, 128 * t:128 * (t + 1)],
                    bankT.ap.rearrange("p (a b) -> p a b", b=128)), bank=bankT)
            else:
                t_trev[t] = ev("vector", lambda E_, t=t, bankT=bankT:
                               E_.tensor_copy(
                                   xnT[:, :, 128 * t:128 * (t + 1)],
                                   bankT.ap.rearrange("p (a b) -> p a b", b=128)),
                               bank=bankT)
        tok_xnT = t_trev[31]
        tok_ln_last = (t_norm[31], t_sq[31])

        t_xnl = None
        for kc in range(8):
            t_r = P.add("vector", lambda E_, kc=kc: E_.tensor_reduce(
                xnl_f, xnT[:, kc, :].rearrange("p (m w) -> p m w", w=16),
                AX.X, ALU.add), wait=[tok_xnT, t_xnl], inc_by=1)
            t_xnl = P.add("gpsimd", lambda E_, kc=kc: E_.tensor_scalar(
                xnlT[:, kc, :], xnl_f, 1.0 / 16.0, None, op0=ALU.mult),
                wait=[t_r], inc_by=1)
        tok_xnl = t_xnl

        if debug_taps:
            dma("sync", "tap", taps["t_xnT"], xnT.rearrange("p a b -> p (a b)"),
                wait=[tok_xnT])
            dma("sync", "tap", taps["t_xnlT"], xnlT.rearrange("p a b -> p (a b)"),
                wait=[tok_xnl])

        t_wq_lastread = [None, None]
        t_attn_dma = []
        t_res_free = [None, None]
        tok_a_pe_end = None
        cdiag_free = None
        s1_e1_free = [[None] * 4, [None] * 4]
        s1_s_free = [[None] * 2, [None] * 2]
        s1_u_free = [[None] * 2, [None] * 2]

        def emit_interleaved(gens):
            gens = [g for g in gens]
            while gens:
                nxt = []
                for g in gens:
                    try:
                        next(g)
                        nxt.append(g)
                    except StopIteration:
                        pass
                gens = nxt

        for p in range(4):
            pp = p % 2
            pair_ew_bar = [("scalar", P.counts["scalar"]),
                           ("gpsimd", P.counts["gpsimd"])]
            wq = wq_pp[pp]
            wload = [t_wq_lastread[pp], tok_consts]
            t_wq = dma("gpsimd", "qw0" if pp == 0 else "qw1",
                       wq, d_wqkv[p], wait=wload)
            t_qk_ev = None
            for dst, cofs, qs in [(qT2, 0, QS), (kT2, 128, None)]:
                for n in range(8):
                    bank = pA if n % 2 == 0 else pB
                    for kc in range(8):
                        mm(bank, wq[:, kc, cofs:cofs + 128],
                           xnT[:, kc, n * 512:(n + 1) * 512],
                           start=(kc == 0), stop=(kc == 7),
                           wait=[t_wq, t_trev[4 * n + 3]] if kc == 0 else (),
                           inc=(kc == 7))
                    if qs is None:
                        t_qk_ev = ev("vector", lambda E_, dst=dst, n=n, bank=bank:
                                     E_.tensor_copy(dst[:, n * 512:(n + 1) * 512],
                                                    bank.ap), bank=bank)
                    else:
                        t_qk_ev = ev("vector", lambda E_, dst=dst, n=n, bank=bank, qs=qs:
                                     E_.tensor_scalar(dst[:, n * 512:(n + 1) * 512],
                                                      bank.ap, qs, None, op0=ALU.mult),
                                     bank=bank)
            t_v_ev = None
            for j in range(2):
                for mg in range(4):
                    bank = pC if mg % 2 == 0 else pD
                    for ms in range(8):
                        mchunk = mg * 8 + ms
                        for kc in range(8):
                            mm(bank, xnT[:, kc, mchunk * 128:(mchunk + 1) * 128],
                               wq[:, kc, 256 + j * 64:256 + j * 64 + 64],
                               start=(kc == 0), stop=(kc == 7),
                               out=bank.ap[:, ms * 64:(ms + 1) * 64],
                               wait=[t_wq, t_trev[mg * 8 + 7]] if (ms == 0 and kc == 0) else (),
                               inc=(kc == 7 and ms == 7))
                    t_v_ev = ev("vector", lambda E_, j=j, mg=mg, bank=bank:
                                E_.tensor_copy(
                                    vv2v[:, j, mg * 8:(mg + 1) * 8, 0:64],
                                    bank.ap.rearrange("p (a b) -> p a b", b=64)),
                                bank=bank)
            for dst, cofs, qs in [(qlT2, 0, QS), (klT2, 128, None)]:
                for kc in range(8):
                    mm(pS, wq[:, kc, cofs:cofs + 128], xnlT[:, kc, :],
                       start=(kc == 0), stop=(kc == 7), out=pS.ap[:, 0:256],
                       wait=[tok_xnl] if kc == 0 else (), inc=(kc == 7))
                if qs is None:
                    ev("vector", lambda E_, dst=dst: E_.tensor_copy(
                        dst[:], pS.ap[:, 0:256]), bank=pS)
                else:
                    ev("vector", lambda E_, dst=dst, qs=qs: E_.tensor_scalar(
                        dst[:], pS.ap[:, 0:256], qs, None, op0=ALU.mult), bank=pS)
            t_lmk_ev = ("vector", P.counts["vector"])
            t_cs = mm(pS, pairsel_t[:], wconv_t[:, p * KC:(p + 1) * KC], True, True,
                      wait=[tok_consts, t_res_free[0]], out=pS.ap[:, 0:KC], inc=1)
            t_csv = ev("vector", lambda E_: E_.tensor_copy(convS[:], pS.ap[:, 0:KC]),
                       bank=pS, wait=pair_ew_bar + [t_res_free[0]])
            t_wq_lastread[pp] = ("tensor", P.counts["tensor"])

            HS = [dict(), dict()]
            for j in range(2):
                HS[j]["C"] = hc[j]
                HS[j]["m3c"] = m3c[j]
                HS[j]["mp"] = mp2[j]
                HS[j]["out3b"] = out3b2[j]
                HS[j]["rrb"] = rrb2[j]
                HS[j]["ql"] = qlT2[64 * j:64 * j + 64, :]
                HS[j]["kl"] = klT2[64 * j:64 * j + 64, :]
                HS[j]["qTh"] = qT2[64 * j:64 * j + 64, :]
                HS[j]["kTh"] = kT2[64 * j:64 * j + 64, :]

            def sim2_z0(j, bar):
                h = HS[j]
                C, mpb = h["C"], h["mp"]
                ql, kl = h["ql"], h["kl"]
                for mc in range(2):
                    mm(pS, ql[:, mc * 128:(mc + 1) * 128], kl, True, True,
                       out=pS.ap[:, mc * 256:(mc + 1) * 256],
                       wait=bar if mc == 0 else (), inc=(mc == 1))
                t_aev = None
                for mc in range(2):
                    t_mx = P.add("vector", lambda E_, mc=mc, C=C: E_.reduce_max(
                        C[:, 0 + mc:1 + mc], pS.ap[:, mc * 256:(mc + 1) * 256],
                        axis=AX.X, negate=True), wait=[pS.prod], inc_by=1)
                    t_aev = P.add("scalar", lambda E_, mc=mc, C=C: E_.activation(
                        mpb["a"][:, mc, :], pS.ap[:, mc * 256:(mc + 1) * 256], AF.Exp,
                        bias=C[:, 0 + mc:1 + mc], accum_out=C[:, 4 + mc:5 + mc]),
                        wait=[t_mx], inc_by=1)
                pS.cons = t_aev
                t_rc = P.add("vector", lambda E_, C=C: E_.reciprocal(
                    C[:, 6:8], C[:, 4:6]), wait=[t_aev], inc_by=1)
                t_an = None
                for mc in range(2):
                    t_an = P.add("scalar", lambda E_, mc=mc, C=C, mpb=mpb: E_.activation(
                        mpb["a"][:, mc, :], mpb["a"][:, mc, :], AF.Copy,
                        scale=C[:, 6 + mc:7 + mc]), wait=[t_rc], inc_by=1)
                bankT = pT1
                for kc in range(2):
                    for jj in range(2):
                        trans(bankT, mpb["a"][:, jj, kc * 128:(kc + 1) * 128],
                              out=bankT.ap[:, (kc * 2 + jj) * 128:(kc * 2 + jj + 1) * 128],
                              wait=[t_an] if (kc == 0 and jj == 0) else (),
                              inc=(kc == 1 and jj == 1))
                t_aT = ev("scalar", lambda E_, mpb=mpb: E_.copy(
                    mpb["aT"].rearrange("p a b -> p (a b)"), bankT.ap[:, 0:512]),
                    bank=bankT)
                h["t_aT"] = t_aT
                h["t_an"] = t_an
                for jj in range(2):
                    mm(pH, ones_col[:], mpb["a"][:, jj, :], start=(jj == 0), stop=(jj == 1),
                       out=pH.ap[0:1, 0:256], wait=[t_an] if jj == 0 else (), inc=(jj == 1))
                for kc in range(2):
                    mm(pH, ones_col[:], mpb["aT"][:, kc, :], start=(kc == 0), stop=(kc == 1),
                       out=pH.ap[0:1, 256:512], wait=[t_aT] if kc == 0 else (), inc=(kc == 1))
                t_c1 = P.add("vector", lambda E_: E_.reduce_max(
                    onebuf[0:1, 0:1], pH.ap[0:1, 0:256], axis=AX.X),
                    wait=[pH.prod], inc_by=1)
                t_c2 = P.add("vector", lambda E_: E_.reduce_max(
                    onebuf[0:1, 1:2], pH.ap[0:1, 256:512], axis=AX.X),
                    wait=[pH.prod], inc_by=1)
                pH.cons = t_c2
                t_c3 = P.add("vector", lambda E_: E_.tensor_tensor(
                    onebuf[0:1, 2:3], onebuf[0:1, 0:1], onebuf[0:1, 1:2], op=ALU.mult),
                    wait=[t_c1, t_c2], inc_by=1)
                t_c4 = P.add("vector", lambda E_: E_.reciprocal(
                    onebuf[0:1, 3:4], onebuf[0:1, 2:3]), wait=[t_c3], inc_by=1)
                t_c5 = P.add("vector", lambda E_: E_.tensor_copy(
                    sbc[0:1, 0:1], onebuf[0:1, 3:4]), wait=[t_c4], inc_by=1)
                t_bc = mm(pH, ones_row[:], sbc[0:1, 0:1], True, True,
                          out=pH.ap[:, 320:321], wait=[t_c5], inc=1)
                t_scol = ev("vector", lambda E_, C=C: E_.tensor_copy(
                    C[:, 8:9], pH.ap[:, 320:321]), bank=pH)
                h["t_z"] = P.add("scalar", lambda E_, C=C, mpb=mpb: E_.activation(
                    mpb["zA"].rearrange("p a b -> p (a b)"),
                    mpb["aT"].rearrange("p a b -> p (a b)"), AF.Copy,
                    scale=C[:, 8:9]), wait=[t_scol, t_aT], inc_by=1)

            def mp_iters(j, bM, bU, bT_):
                h = HS[j]
                mpb = h["mp"]
                t_z = h["t_z"]
                t_aT = h["t_aT"]
                zin, zout = "zA", "zB"
                for it in range(ITERS):
                    for mc in range(2):
                        for kc in range(2):
                            mm(bM, mpb["aT"][:, kc, mc * 128:(mc + 1) * 128],
                               mpb[zin][:, kc, :], start=(kc == 0), stop=(kc == 1),
                               out=bM.ap[:, mc * 256:(mc + 1) * 256],
                               wait=[t_z, t_aT] if (mc == 0 and kc == 0) else (),
                               inc=(mc == 1 and kc == 1))
                    t_az = ev("scalar", lambda E_, mpb=mpb: E_.copy(
                        mpb["az"].rearrange("p a b -> p (a b)"), bM.ap[:]), bank=bM)
                    yield
                    for kc in range(2):
                        for jj in range(2):
                            trans(bT_, mpb["az"][:, jj, kc * 128:(kc + 1) * 128],
                                  out=bT_.ap[:, (kc * 2 + jj) * 128:(kc * 2 + jj + 1) * 128],
                                  wait=[t_az] if (kc == 0 and jj == 0) else (),
                                  inc=(kc == 1 and jj == 1))
                    t_azT = ev("scalar", lambda E_, mpb=mpb: E_.activation(
                        mpb["azTn"].rearrange("p a b -> p (a b)"), bT_.ap[:, 0:512],
                        AF.Copy, scale=-1.0), bank=bT_)
                    yield
                    for mc in range(2):
                        mm(bU, eye7, mpb["az"][:, mc, :], start=True, stop=False,
                           out=bU.ap[:, mc * 256:(mc + 1) * 256],
                           wait=[t_az, t_azT] if mc == 0 else ())
                        for kc in range(2):
                            mm(bU, mpb["azTn"][:, kc, mc * 128:(mc + 1) * 128],
                               mpb["az"][:, kc, :], start=False, stop=(kc == 1),
                               out=bU.ap[:, mc * 256:(mc + 1) * 256],
                               inc=(mc == 1 and kc == 1))
                    t_u = ev("vector", lambda E_, mpb=mpb: E_.tensor_copy(
                        mpb["u"].rearrange("p a b -> p (a b)"), bU.ap[:]), bank=bU)
                    yield
                    for mc in range(2):
                        mm(bM, eye15, mpb["az"][:, mc, :], start=True, stop=False,
                           out=bM.ap[:, mc * 256:(mc + 1) * 256],
                           wait=[t_u] if mc == 0 else ())
                        for kc in range(2):
                            mm(bM, mpb["azTn"][:, kc, mc * 128:(mc + 1) * 128],
                               mpb["u"][:, kc, :], start=False, stop=(kc == 1),
                               out=bM.ap[:, mc * 256:(mc + 1) * 256],
                               inc=(mc == 1 and kc == 1))
                    t_w2 = ev("scalar", lambda E_, mpb=mpb: E_.copy(
                        mpb["w2"].rearrange("p a b -> p (a b)"), bM.ap[:]), bank=bM)
                    yield
                    for kc in range(2):
                        for jj in range(2):
                            trans(bT_, mpb[zin][:, jj, kc * 128:(kc + 1) * 128],
                                  out=bT_.ap[:, (kc * 2 + jj) * 128:(kc * 2 + jj + 1) * 128],
                                  wait=[t_w2] if (kc == 0 and jj == 0) else (),
                                  inc=(kc == 1 and jj == 1))
                    t_zT = ev("scalar", lambda E_, mpb=mpb: E_.activation(
                        mpb["azTn"].rearrange("p a b -> p (a b)"), bT_.ap[:, 0:512],
                        AF.Copy, scale=-0.25), bank=bT_)
                    yield
                    for mc in range(2):
                        mm(bU, eye325, mpb[zin][:, mc, :], start=True, stop=False,
                           out=bU.ap[:, mc * 256:(mc + 1) * 256],
                           wait=[t_zT] if mc == 0 else ())
                        for kc in range(2):
                            mm(bU, mpb["azTn"][:, kc, mc * 128:(mc + 1) * 128],
                               mpb["w2"][:, kc, :], start=False, stop=(kc == 1),
                               out=bU.ap[:, mc * 256:(mc + 1) * 256],
                               inc=(mc == 1 and kc == 1))
                    t_z = ev("vector", lambda E_, mpb=mpb, zout=zout: E_.tensor_copy(
                        mpb[zout].rearrange("p a b -> p (a b)"), bU.ap[:]), bank=bU)
                    zin, zout = zout, zin
                    yield
                for kc in range(2):
                    for jj in range(2):
                        trans(bT_, mpb[zin][:, jj, kc * 128:(kc + 1) * 128],
                              out=bT_.ap[:, (kc * 2 + jj) * 128:(kc * 2 + jj + 1) * 128],
                              wait=[t_z] if (kc == 0 and jj == 0) else (),
                              inc=(kc == 1 and jj == 1))
                h["t_wT"] = ev("scalar", lambda E_, mpb=mpb: E_.copy(
                    mpb["wT"].rearrange("p a b -> p (a b)"), bT_.ap[:, 0:512]), bank=bT_)

            def sim3_out3(j, b1, b2, o3a, o3b_):
                h = HS[j]
                C = h["C"]
                ql, kTh = h["ql"], h["kTh"]
                t_exs = []
                for tc in range(32):
                    bank = b1 if (tc // 2) % 2 == 0 else b2
                    half = tc % 2
                    mm(bank, kTh[:, tc * 128:(tc + 1) * 128], ql, True, True,
                       out=bank.ap[:, half * 256:(half + 1) * 256],
                       wait=[t_lmk_ev] if tc == 0 else (), inc=1)
                    t_exs.append(ev(
                        "scalar", lambda E_, tc=tc, bank=bank, half=half:
                        E_.activation(e3T[:, tc, :],
                                      bank.ap[:, half * 256:(half + 1) * 256],
                                      AF.Exp), bank=bank))
                    if tc % 4 == 3:
                        yield
                o3bank = [o3a, o3b_]
                for m in range(32):
                    for mc in range(2):
                        mm(o3bank[mc], e3T[:, m, mc * 128:(mc + 1) * 128],
                           vv2v[:, j, m, 0:65],
                           start=(m == 0), stop=(m == 31),
                           out=o3bank[mc].ap[:, 0:65],
                           wait=[t_exs[m]] +
                           ([t_v_ev, tok_vones] if (m == 0 and mc == 0) else []),
                           inc=(m == 31))
                    if m % 8 == 7:
                        yield
                t_o3 = None
                for mc in range(2):
                    t_rc3 = P.add("vector", lambda E_, mc=mc, C=C, bk=o3bank[mc]:
                                  E_.reciprocal(C[:, 13 + mc:14 + mc],
                                                bk.ap[:, 64:65]),
                                  wait=[o3bank[mc].prod], inc_by=1)
                    t_o3 = ev("scalar", lambda E_, mc=mc, C=C, j=j, bk=o3bank[mc]:
                              E_.activation(HS[j]["out3b"][:, mc, :],
                                            bk.ap[:, 0:64], AF.Copy,
                                            scale=C[:, 13 + mc:14 + mc]),
                              bank=o3bank[mc], wait=[t_rc3])
                h["t_o3"] = t_o3

            def rr_sim1(j, b1, b2, t_conv_prev):
                h = HS[j]
                mpb, rrb_ = h["mp"], h["rrb"]
                qTh, kl = h["qTh"], h["kl"]
                e1t = e1T2[j]
                P0 = 64 * j
                PS_ = 64 - P0
                bO = [pH, pS]
                for mc in range(2):
                    for kc in range(2):
                        mm(pS, mpb["wT"][:, kc, mc * 128:(mc + 1) * 128],
                           HS[j]["out3b"][:, kc, :], start=(kc == 0), stop=(kc == 1),
                           out=pS.ap[:, mc * 64:(mc + 1) * 64],
                           wait=[h["t_wT"], h["t_o3"]] if (mc == 0 and kc == 0) else (),
                           inc=(mc == 1 and kc == 1))
                t_rr = ev("vector", lambda E_, rrb_=rrb_: E_.tensor_copy(
                    rrb_.rearrange("p a b -> p (a b)"),
                    pS.ap[:, 0:128]), bank=pS)
                yield
                for g in range(8):
                    sl = g % 2
                    t_e = [None, None]
                    for mc, bank in ((0, b1), (1, b2)):
                        mm(bank, kl[:, mc * 128:(mc + 1) * 128],
                           qTh[:, g * 512:(g + 1) * 512], start=True, stop=True,
                           wait=[t_lmk_ev] if (g == 0 and mc == 0) else (), inc=1)
                        t_e[mc] = ev("scalar", lambda E_, sl=sl, mc=mc, bank=bank,
                                     e1t=e1t: E_.activation(
                                         e1t[:, sl * 2 + mc, :], bank.ap, AF.Exp),
                                     bank=bank, wait=[s1_e1_free[j][sl * 2 + mc]])
                    bo = bO[sl]
                    for kc in range(2):
                        mm(bo, rrb_[:, kc, :], e1t[:, sl * 2 + kc, :],
                           start=(kc == 0), stop=(kc == 1),
                           out=bo.ap[P0:P0 + 64, 0:512],
                           wait=([t_rr, t_e[0], t_e[1], tok_rrones]
                                 if kc == 0 else ()),
                           inc=(kc == 1))
                    for kc in range(2):
                        mm(bo, ones_col[:], e1t[:, sl * 2 + kc, :],
                           start=(kc == 0), stop=(kc == 1),
                           out=bo.ap[PS_:PS_ + 1, 0:512],
                           inc=(kc == 1))
                    s1_e1_free[j][sl * 2 + 0] = bo.prod
                    s1_e1_free[j][sl * 2 + 1] = bo.prod
                    def _recip_bf(E_, bo=bo, PS_=PS_):
                        with nc.allow_low_precision(
                                reason="attn1 normalizer row in bf16, matches "
                                       "bf16 storage of normalized attn"):
                            return E_.reciprocal(ssb_sh[PS_:PS_ + 1, 0:512],
                                                 bo.ap[PS_:PS_ + 1, 0:512])
                    t_rc1 = P.add("vector", _recip_bf,
                                  wait=[bo.prod, s1_s_free[j][0]], inc_by=1)
                    t_u = P.add("vector", lambda E_, bo=bo, P0=P0:
                                E_.tensor_copy(usb_sh[P0:P0 + 64, 0:512],
                                               bo.ap[P0:P0 + 64, 0:512]),
                                wait=[bo.prod], inc_by=1)
                    bo.cons = t_u
                    bb = bO[1 - sl]
                    t_bc = mm(bb, onesp[PS_:PS_ + 1, :],
                              ssb_sh[PS_:PS_ + 1, 0:512], True, True,
                              out=bb.ap[P0:P0 + 64, 0:512], wait=[t_rc1], inc=1)
                    s1_s_free[j][0] = t_bc
                    t_mul = ev("vector", lambda E_, g=g, j=j, bb=bb, P0=P0:
                               E_.tensor_tensor(
                                   conv_buf[P0:P0 + 64,
                                            6 + g * 512:6 + (g + 1) * 512],
                                   bb.ap[P0:P0 + 64, 0:512],
                                   usb_sh[P0:P0 + 64, 0:512],
                                   op=ALU.mult), bank=bb,
                               wait=[t_u] + ([t_conv_prev] if g == 0 else []))
                    if g == 7:
                        h["t_oh"] = t_mul
                    yield

            sim2_z0(0, pair_ew_bar)
            emit_interleaved([sim3_out3(0, pA, pB, pS, pH)])
            sim2_z0(1, [])
            emit_interleaved([
                mp_iters(0, pC, pD, pT2),
                sim3_out3(1, pA, pB, pS, pH),
            ])
            emit_interleaved([
                mp_iters(1, pC, pD, pT2),
                rr_sim1(0, pA, pB, t_res_free[0]),
            ])
            emit_interleaved([rr_sim1(1, pC, pD, t_res_free[0])])
            tok_a_pe_end = ("tensor", P.counts["tensor"])
            t_oh_ev = HS[1]["t_oh"]
            t_oh_ev0 = HS[0]["t_oh"]

            t_oh_all = [t_oh_ev, t_oh_ev0]
            t_diag = None
            for tap in range(KC):
                t_diag = P.add("vector", lambda E_, tap=tap: E_.tensor_scalar(
                    cdiag[:, tap, :], eye, convS[:, tap:tap + 1], None, op0=ALU.mult),
                    wait=[t_csv, tok_consts, (cdiag_free if p > 0 else None)], inc_by=1)
            for hf in range(2):
                base = hf * 2048
                for nn in range(4):
                    bank = pC if nn % 2 == 0 else pD
                    for tap in range(KC):
                        mm(bank, cdiag[:, tap, :],
                           conv_buf[:, base + nn * 512 + tap:base + nn * 512 + tap + 512],
                           start=(tap == 0), stop=(tap == KC - 1),
                           wait=(t_oh_all + [t_diag, t_cbz, t_cbz2,
                                             t_res_free[0], t_res_free[1]])
                           if (hf == 0 and nn == 0 and tap == 0) else (),
                           inc=(tap == KC - 1))
                    ev("vector", lambda E_, base=base, nn=nn, bank=bank:
                       E_.tensor_tensor(res_pp[0][0:128, nn * 512:(nn + 1) * 512],
                                        bank.ap,
                                        conv_buf[:, 6 + base + nn * 512:
                                                 6 + base + (nn + 1) * 512],
                                        op=ALU.add), bank=bank,
                       wait=[t_res_free[1]] if nn == 0 else ())
                t_fin = ("vector", P.counts["vector"])
                t_st = dma("sync", "ast",
                           i_attnT.ap()[128 * p:128 * (p + 1), base:base + 2048],
                           res_pp[0], wait=[t_fin])
                t_attn_dma.append(t_st)
                t_res_free = [t_fin, t_st]
            cdiag_free = ("tensor", P.counts["tensor"])

        t_ag = None
        for p4 in range(4):
            t_ag = P.add("gpsimd", lambda E_, p4=p4: E_.collective_compute(
                "AllGather", ALU.bypass,
                replica_groups=[[0, 1], [2, 3], [4, 5], [6, 7]],
                ins=[i_attnT.ap()[128 * p4:128 * (p4 + 1), :].opt()],
                outs=[i_pairT.ap()[256 * p4:256 * (p4 + 1), :].opt()]),
                wait=t_attn_dma[2 * p4:2 * p4 + 2], inc_by=1, sem="cc")
        if debug_taps:
            dma("sync", "tap", taps["t_attnT"], i_attnT.ap(), wait=t_attn_dma)

        def _load_pair(E_, kc):
            pid = pid_cell["pid"]
            return E_.dma_start(out=pair_own[:, kc, :],
                                in_=i_pairT.ap()[128 * kc:128 * (kc + 1),
                                                 bass.ts(pid % 2, TOK)])
        tok_pair = None
        t_po = []
        for kc in range(8):
            tok_pair = P.add("scalar", lambda E_, kc=kc: _load_pair(E_, kc),
                             wait=[("cc", kc // 2 + 1), tok_a_pe_end,
                                   ("vector", P.counts["vector"]),
                                   ("gpsimd", P.counts["gpsimd"])],
                             inc_by=16, sem="spr")
            t_po.append(tok_pair)

        BAR_A = P.all_toks()
        t_wfc = dma("gpsimd", "gw", wfc_bf,
                    d_wfc.rearrange("(c p) n -> p c n", p=128), wait=BAR_A)
        t_bfcr = dma("sync", "cst", bfc_row[:], d_bfc, wait=BAR_A)
        t_bfc = None
        for nn in range(2):
            bank = pA if nn == 0 else pB
            mm(bank, ones_rowf[:], bfc_row[0:1, nn * 512:(nn + 1) * 512], True, True,
               wait=[t_bfcr, t_rwl] if nn == 0 else (), inc=1)
            t_bfc = ev("vector", lambda E_, nn=nn, bank=bank: E_.tensor_copy(
                bfc_t[:, nn * 512:(nn + 1) * 512], bank.ap), bank=bank)

        pre_wg = dma("gpsimd", "gw", wgt,
                     d_wgT[0].rearrange("(c p) n -> p c n", p=128), wait=BAR_A)
        pre_wu = dma("gpsimd", "gw", wut,
                     d_wuT[0].rearrange("(c p) n -> p c n", p=128), wait=BAR_A)
        t_xn2ev = [None] * 16
        t_gates = [None] * 16
        t_xmid_dma = [None] * 16
        lncolB = [smc[:, 0:8], smc[:, 8:16], smcB[:, 0:8], smcB[:, 8:16]]
        prev_lo = [None] * 4
        prev_sq2 = [None] * 4
        prev_loev = [None, None]
        prev_gates = None
        for m in range(16):
            xt = xtB[m % 4]
            lc = lncolB[m % 4]
            xm = xt
            w = list(BAR_A) if m < 4 else [prev_lo[m % 4], prev_sq2[m % 4]]
            t_xt = dma("sync", "xA0" if m % 2 == 0 else "xA1",
                       xt, d_xo[128 * m:128 * (m + 1), :], wait=w)
            fcb = [None, None]
            fcpair = (pA, pB) if m % 2 == 0 else (pC, pD)
            for nn in range(2):
                bank = fcpair[nn]
                for kc in range(8):
                    mm(bank, pair_own[:, kc, m * 128:(m + 1) * 128],
                       wfc_bf[:, kc, nn * 512:(nn + 1) * 512],
                       start=(kc == 0), stop=(kc == 7),
                       wait=[t_po[kc], t_wfc] if m == 0 else (),
                       inc=(kc == 7))
                fcb[nn] = bank
            t_xb = P.add("gpsimd", lambda E_, xt=xt: E_.tensor_tensor(
                xt, bfc_t, xt, op=ALU.add), wait=[t_xt, t_bfc], inc_by=1)
            t_x1 = None
            for nn in range(2):
                bank = fcb[nn]
                t_x1 = ev("vector", lambda E_, nn=nn, bank=bank, xt=xt:
                          E_.scalar_tensor_tensor(
                              xt[:, nn * 512:(nn + 1) * 512], bank.ap, 1.0,
                              xt[:, nn * 512:(nn + 1) * 512],
                              op0=ALU.mult, op1=ALU.add),
                          bank=bank, wait=[t_xb])
            t_x2 = t_x1
            t_xmid_dma[m] = dma("sync", "xm0" if m % 2 == 0 else "xm1",
                                i_xmid[128 * m:128 * (m + 1), :], xm, wait=[t_x2])
            t_xmtap = None
            if debug_taps:
                t_xmtap = dma("sync", "tap", taps["t_xmid"][128 * m:128 * (m + 1), :],
                              xm, wait=[t_x2])
            t_sum = P.add("vector", lambda E_, xm=xm, lc=lc: E_.reduce_sum(
                lc[:, 0:1], xm, axis=AX.X), wait=[t_x2], inc_by=1)
            t_sq2 = P.add("scalar", lambda E_, xm=xm, lc=lc: E_.activation(
                sq_f32, xm, AF.Square, accum_out=lc[:, 1:2]), wait=[t_x2], inc_by=1)
            prev_sq2[m % 4] = t_sq2
            t_q = P.add("vector", lambda E_, lc=lc: E_.scalar_tensor_tensor(
                lc[:, 2:3], lc[:, 0:1], -1.0 / D, lc[:, 0:1], op0=ALU.mult, op1=ALU.mult),
                wait=[t_sum], inc_by=1)
            t_var = P.add("vector", lambda E_, lc=lc: E_.tensor_tensor(
                lc[:, 4:5], lc[:, 1:2], lc[:, 2:3], op=ALU.add),
                wait=[t_q, t_sq2], inc_by=1)
            t_ln = P.add("scalar", lambda E_, lc=lc: E_.activation(
                lc[:, 5:6], lc[:, 4:5], AF.Ln, bias=epsc, scale=1.0 / D),
                wait=[t_var], inc_by=1)
            t_rs = P.add("scalar", lambda E_, lc=lc: E_.activation(
                lc[:, 6:7], lc[:, 5:6], AF.Exp, scale=-0.5), wait=[t_ln], inc_by=1)
            t_nb = P.add("vector", lambda E_, lc=lc: E_.scalar_tensor_tensor(
                lc[:, 7:8], lc[:, 0:1], -1.0 / D, lc[:, 6:7], op0=ALU.mult, op1=ALU.mult),
                wait=[t_rs], inc_by=1)
            xb2 = xn_st[m % 2]
            t_nbf = P.add("gpsimd", lambda E_, xm=xm, lc=lc, xb2=xb2: E_.tensor_scalar(
                xb2, xm, lc[:, 6:7], lc[:, 7:8], op0=ALU.mult, op1=ALU.add),
                wait=[t_nb, t_xn2ev[m - 2] if m >= 2 else None], inc_by=1)
            t_nf = P.add("scalar", lambda E_, xm=xm, lc=lc, xt=xt: E_.activation(
                xt, xm, AF.Identity, scale=lc[:, 6:7], bias=lc[:, 7:8]),
                wait=[t_nb, t_nbf, t_x1, t_xmid_dma[m], t_xmtap], inc_by=1)
            lo_t = res_pp[m % 2]
            t_lo = P.add("gpsimd", lambda E_, xt=xt, xb2=xb2, lo_t=lo_t: E_.tensor_tensor(
                lo_t[0:128, 0:1024], xt, xb2, op=ALU.subtract),
                wait=[t_nf, t_nbf, prev_loev[m % 2]] +
                     (t_res_free if m < 2 else []), inc_by=1)
            prev_lo[m % 4] = t_lo
            for kc in range(8):
                trans(pT1, xb2[0:128, kc * 128:(kc + 1) * 128],
                      out=pT1.ap[:, kc * 128:(kc + 1) * 128],
                      wait=[t_nbf] if kc == 0 else (), inc=(kc == 7))
            t_xn2ev[m] = ev("scalar", lambda E_, m=m: E_.copy(
                xn2T[:, :, m * 128:(m + 1) * 128],
                pT1.ap.rearrange("p (a b) -> p a b", b=128)), bank=pT1)
            for kc in range(8):
                trans(pT2, lo_t[0:128, kc * 128:(kc + 1) * 128],
                      out=pT2.ap[:, kc * 128:(kc + 1) * 128],
                      wait=[t_lo] if kc == 0 else (), inc=(kc == 7))
            t_loev = ev("scalar", lambda E_: E_.copy(
                xloT_st.rearrange("p a b -> p (a b)"), pT2.ap[:]), bank=pT2)
            prev_loev[m % 2] = t_loev
            nmm = 0
            for w_ in (rw_hi, rw_lo):
                for kc in range(8):
                    mm(pS, xn2T[:, kc, m * 128:(m + 1) * 128], w_[:, kc, :],
                       start=(nmm == 0), stop=False, out=pS.ap[:, 0:8],
                       wait=[t_xn2ev[m]] if nmm == 0 else ())
                    nmm += 1
            for kc in range(8):
                mm(pS, xloT_st[:, kc, :], rw_hi[:, kc, :],
                   start=False, stop=(kc == 7), out=pS.ap[:, 0:8],
                   wait=[t_loev] if kc == 0 else (), inc=(kc == 7))
            lgt = lgt2[m % 2]
            sxv = sxv2[m % 2]
            prev_gates = t_gates[m - 2] if m >= 2 else None
            t_lg = ev("vector", lambda E_, lgt=lgt: E_.tensor_copy(
                lgt[:], pS.ap[:, 0:8]), bank=pS, wait=[prev_gates])
            if debug_taps:
                P.add("vector", lambda E_, m=m: E_.tensor_copy(
                    taps_lgt_sb[:, m, :], lgt[:]), wait=[t_lg], inc_by=1)
            V, G_ = "vector", "gpsimd"
            t1 = P.add(V, lambda E_, sxv=sxv, lgt=lgt: E_.reduce_max(sxv["tmp"][:, 0:1], lgt[:], axis=AX.X),
                       wait=[t_lg], inc_by=1)
            t5a = P.add(V, lambda E_, sxv=sxv, lgt=lgt: E_.tensor_scalar(
                sxv["d"][:], lgt[:], sxv["tmp"][:, 0:1], None, op0=ALU.subtract),
                wait=[t1, prev_gates], inc_by=1)
            t2 = P.add(G_, lambda E_, sxv=sxv, lgt=lgt: E_.tensor_scalar(
                sxv["ind1"][:], sxv["d"][:], 0.0, None, op0=ALU.is_ge),
                wait=[t5a], inc_by=1)
            t3 = P.add("scalar", lambda E_, sxv=sxv, lgt=lgt: E_.activation(
                sxv["absS"][:], lgt[:], AF.Abs),
                wait=[t_lg, prev_gates], inc_by=1)
            t4 = P.add(V, lambda E_, sxv=sxv, lgt=lgt: E_.tensor_scalar(
                sxv["f1"][:], sxv["absS"][:], sxv["tmp"][:, 0:1], None, op0=ALU.max),
                wait=[t3, t1], inc_by=1)
            t5 = t5a
            t6 = P.add(V, lambda E_, sxv=sxv, lgt=lgt: E_.scalar_tensor_tensor(
                sxv["t"][:], sxv["f1"][:], 0.02, sxv["d"][:], op0=ALU.mult, op1=ALU.add),
                wait=[t4, t5], inc_by=1)
            t7 = P.add(V, lambda E_, sxv=sxv, lgt=lgt: E_.tensor_scalar(
                sxv["u1"][:], sxv["t"][:], 0.0, None, op0=ALU.is_ge), wait=[t6], inc_by=1)
            t8 = P.add(V, lambda E_, sxv=sxv, lgt=lgt: E_.tensor_scalar(
                sxv["tmp"][:, 1:2], sxv["tmp"][:, 0:1], -1.0, None, op0=ALU.mult),
                wait=[t1], inc_by=1)
            t9 = P.add("scalar", lambda E_, sxv=sxv, lgt=lgt: E_.activation(
                sxv["ebuf"][:], lgt[:], AF.Exp, bias=sxv["tmp"][:, 1:2]),
                wait=[t8, t_lg, prev_gates], inc_by=1)
            t10 = P.add(G_, lambda E_, sxv=sxv, lgt=lgt: E_.tensor_tensor(
                sxv["eu"][:], sxv["ebuf"][:], sxv["u1"][:], op=ALU.mult),
                wait=[t9, t7], inc_by=1)
            t11 = P.add(V, lambda E_, sxv=sxv, lgt=lgt: E_.reduce_sum(
                sxv["tmp"][:, 2:3], sxv["eu"][:], axis=AX.X), wait=[t10], inc_by=1)
            t12 = P.add(V, lambda E_, sxv=sxv, lgt=lgt: E_.reciprocal(
                sxv["tmp"][:, 3:4], sxv["tmp"][:, 2:3]), wait=[t11], inc_by=1)
            t13 = P.add(V, lambda E_, sxv=sxv, lgt=lgt: E_.scalar_tensor_tensor(
                sxv["ms"][:], sxv["ind1"][:], -1e30, lgt[:], op0=ALU.mult, op1=ALU.add),
                wait=[t2, prev_gates], inc_by=1)
            t14 = P.add(V, lambda E_, sxv=sxv, lgt=lgt: E_.reduce_max(
                sxv["tmp"][:, 4:5], sxv["ms"][:], axis=AX.X), wait=[t13], inc_by=1)
            t15a = P.add(V, lambda E_, sxv=sxv, lgt=lgt: E_.tensor_scalar(
                sxv["d2"][:], sxv["ms"][:], sxv["tmp"][:, 4:5], None, op0=ALU.subtract),
                wait=[t14], inc_by=1)
            t15 = P.add(G_, lambda E_, sxv=sxv, lgt=lgt: E_.tensor_scalar(
                sxv["ind2"][:], sxv["d2"][:], 0.0, None, op0=ALU.is_ge),
                wait=[t15a], inc_by=1)
            t16 = P.add(V, lambda E_, sxv=sxv, lgt=lgt: E_.tensor_scalar(
                sxv["f1"][:], sxv["absS"][:], sxv["tmp"][:, 4:5], None, op0=ALU.max),
                wait=[t14, t6], inc_by=1)
            t17 = P.add(V, lambda E_, sxv=sxv, lgt=lgt: E_.tensor_scalar(
                sxv["d"][:], lgt[:], sxv["tmp"][:, 4:5], None, op0=ALU.subtract),
                wait=[t14, t6], inc_by=1)
            t18 = P.add(V, lambda E_, sxv=sxv, lgt=lgt: E_.scalar_tensor_tensor(
                sxv["t"][:], sxv["f1"][:], 0.02, sxv["d"][:], op0=ALU.mult, op1=ALU.add),
                wait=[t16, t17, t7], inc_by=1)
            t19 = P.add(V, lambda E_, sxv=sxv, lgt=lgt: E_.tensor_scalar(
                sxv["u1"][:], sxv["t"][:], 0.0, None, op0=ALU.is_ge),
                wait=[t18, t10], inc_by=1)
            t20 = P.add(V, lambda E_, sxv=sxv, lgt=lgt: E_.tensor_scalar(
                sxv["tmp"][:, 5:6], sxv["tmp"][:, 4:5], -1.0, None, op0=ALU.mult),
                wait=[t14, t9], inc_by=1)
            t21 = P.add("scalar", lambda E_, sxv=sxv, lgt=lgt: E_.activation(
                sxv["ebuf"][:], sxv["ms"][:], AF.Exp, bias=sxv["tmp"][:, 5:6]),
                wait=[t20, t10], inc_by=1)
            t22 = P.add(G_, lambda E_, sxv=sxv, lgt=lgt: E_.tensor_tensor(
                sxv["eu"][:], sxv["ebuf"][:], sxv["u1"][:], op=ALU.mult),
                wait=[t21, t19], inc_by=1)
            t23 = P.add(V, lambda E_, sxv=sxv, lgt=lgt: E_.reduce_sum(
                sxv["tmp"][:, 6:7], sxv["eu"][:], axis=AX.X), wait=[t22], inc_by=1)
            t24 = P.add(V, lambda E_, sxv=sxv, lgt=lgt: E_.reciprocal(
                sxv["tmp"][:, 7:8], sxv["tmp"][:, 6:7]), wait=[t23], inc_by=1)
            t25 = P.add(G_, lambda E_, sxv=sxv, lgt=lgt: E_.tensor_scalar(
                sxv["ebuf"][:], sxv["ind1"][:], sxv["tmp"][:, 3:4], None, op0=ALU.mult),
                wait=[t12, t21, t22], inc_by=1)
            t_gates[m] = P.add(V, lambda E_, m=m, sxv=sxv: E_.scalar_tensor_tensor(
                gcol[:, m, :], sxv["ind2"][:], sxv["tmp"][:, 7:8], sxv["ebuf"][:],
                op0=ALU.mult, op1=ALU.add), wait=[t24, t25, t15], inc_by=1)
        if debug_taps:
            dma("sync", "tap", taps["t_gates"], gcol.rearrange("p a b -> p (a b)"),
                wait=[t_gates[15]])
            dma("sync", "tap", taps["t_lgt"], taps_lgt_sb.rearrange("p a b -> p (a b)"),
                wait=[("vector", P.counts["vector"])])

        tok_fc_pe = ("tensor", P.counts["tensor"])
        BAR_B2 = P.all_toks()

        prev_gates_pe = None
        prev_down_pe = None
        prev_hev = {}
        t_y_dma = []
        t_oacc_prev = [None] * 8
        prev_yadd = [None]
        for hf in range(2):
            for e_ in range(E):
                wld = [prev_gates_pe]
                if hf == 0 and e_ == 0:
                    t_wg, t_wu = pre_wg, pre_wu
                else:
                    t_wg = dma("gpsimd", "gw", wgt,
                               d_wgT[e_].rearrange("(c p) n -> p c n", p=128), wait=wld)
                    t_wu = dma("gpsimd", "gw", wut,
                               d_wuT[e_].rearrange("(c p) n -> p c n", p=128), wait=wld)
                wldd = [prev_down_pe]
                if hf == 0 and e_ == 0:
                    wldd += BAR_B2
                t_wd = dma("gpsimd", "gd", wdt,
                           d_wdT[e_].rearrange("(c p) n -> p c n", p=128), wait=wldd)
                t_h_ev = None
                for f in range(16):
                    for nn in range(2):
                        bg_ = pA if nn == 0 else pC
                        bu_ = pB if nn == 0 else pD
                        for kc in range(8):
                            mm(bg_, wgt[:, kc, f * 128:(f + 1) * 128],
                               xn2T[:, kc, hf * 1024 + nn * 512:hf * 1024 + (nn + 1) * 512],
                               start=(kc == 0), stop=(kc == 7),
                               wait=[t_wu, tok_fc_pe] if (f == 0 and nn == 0 and kc == 0) else (),
                               inc=(kc == 7))
                        for kc in range(8):
                            mm(bu_, wut[:, kc, f * 128:(f + 1) * 128],
                               xn2T[:, kc, hf * 1024 + nn * 512:hf * 1024 + (nn + 1) * 512],
                               start=(kc == 0), stop=(kc == 7),
                               wait=[t_wu] if (f == 0 and nn == 0 and kc == 0) else (),
                               inc=(kc == 7))
                        t_si = ev("scalar", lambda E_, nn=nn, bg_=bg_: E_.activation(
                            silu_st[nn], bg_.ap, AF.Silu), bank=bg_,
                            wait=[prev_hev.get(nn), (t_bfc if (hf == 0 and e_ == 0) else None)])
                        t_h_ev = ev("vector", lambda E_, f=f, nn=nn, bu_=bu_:
                                    E_.tensor_tensor(
                                        hT[:, f, nn * 512:(nn + 1) * 512],
                                        silu_st[nn], bu_.ap, op=ALU.mult),
                                    bank=bu_, wait=[t_si])
                        prev_hev[nn] = t_h_ev
                prev_gates_pe = ("tensor", P.counts["tensor"])
                for m8 in range(8):
                    mslot = hf * 8 + m8
                    for nn in range(2):
                        bank = pA if nn == 0 else pB
                        for kc in range(16):
                            mm(bank, hT[:, kc, m8 * 128:(m8 + 1) * 128],
                               wdt[:, kc, nn * 512:(nn + 1) * 512],
                               start=(kc == 0), stop=(kc == 15),
                               wait=[t_h_ev, t_wd] if (m8 == 0 and nn == 0 and kc == 0) else (),
                               inc=(kc == 15))
                        if e_ == 0:
                            ev("vector", lambda E_, m8=m8, nn=nn, bank=bank, mslot=mslot:
                               E_.tensor_scalar(
                                   out_acc[:, m8, nn * 512:(nn + 1) * 512], bank.ap,
                                   gcol[:, mslot, 0:1], None, op0=ALU.mult),
                               bank=bank,
                               wait=[t_gates[mslot], t_oacc_prev[m8]])
                        else:
                            ev("vector", lambda E_, m8=m8, nn=nn, bank=bank, mslot=mslot, e_=e_:
                               E_.scalar_tensor_tensor(
                                   out_acc[:, m8, nn * 512:(nn + 1) * 512], bank.ap,
                                   gcol[:, mslot, e_:e_ + 1],
                                   out_acc[:, m8, nn * 512:(nn + 1) * 512],
                                   op0=ALU.mult, op1=ALU.add), bank=bank)
                prev_down_pe = ("tensor", P.counts["tensor"])
            for m8 in range(8):
                mslot = hf * 8 + m8
                t_xr = dma("sync", "fin", xtF, i_xmid[128 * mslot:128 * (mslot + 1), :],
                           wait=[t_xmid_dma[mslot], prev_yadd[0],
                                 ("vector", P.counts["vector"]),
                                 (t_y_dma[-1] if t_y_dma else None)])
                t_add = P.add("vector", lambda E_, m8=m8: E_.tensor_tensor(
                    xtF, xtF, out_acc[:, m8, :], op=ALU.add),
                    wait=[t_xr], inc_by=1)
                t_oacc_prev[m8] = t_add
                prev_yadd[0] = t_add
                t_y_dma.append(dma("sync", "fin", d_y[128 * mslot:128 * (mslot + 1), :],
                                   xtF, wait=[t_add]))

        P.add("gpsimd", lambda E_: E_.engine_nop(), wait=P.all_toks(), inc_by=1)

        with nc.Block() as block:
            P.emit(block, sems)
    return nc


_eye = np.eye(128).astype(BFD)
CONSTS = dict(
    eye_bf=_eye,
    eye7_bf=(7.0 * np.eye(128)).astype(BFD),
    eye15_bf=(15.0 * np.eye(128)).astype(BFD),
    eye325_bf=(3.25 * np.eye(128)).astype(BFD),
    ones_col_bf=np.ones((128, 1), dtype=BFD),
    ones_row_bf=np.ones((1, 128), dtype=BFD),
    ones_row_f=np.ones((1, 128), dtype=np.float32),
    pairsel=np.stack([(np.arange(128) < 64).astype(np.float32),
                      (np.arange(128) >= 64).astype(np.float32)], axis=0),
)


def make_in_maps(L0_attn_x, L0_attn_w_qkv, L0_attn_w_conv, L0_attn_w_fc,
                 L0_attn_b_fc, L0_ffn_router_w, L0_ffn_w_gate, L0_ffn_w_up,
                 L0_ffn_w_down):
    x = np.ascontiguousarray(np.asarray(L0_attn_x, dtype=np.float32))
    wqkv = np.asarray(L0_attn_w_qkv, dtype=np.float32)
    wconv = np.asarray(L0_attn_w_conv, dtype=np.float32).reshape(16, KC)
    wfc0 = np.asarray(L0_attn_w_fc, dtype=np.float32)
    head_perm = []
    for pr in range(4):
        head_perm += [2 * pr, 2 * pr + 1, 8 + 2 * pr, 8 + 2 * pr + 1]
    row_perm = np.concatenate([np.arange(64) + 64 * h for h in head_perm])
    wfc = np.ascontiguousarray(wfc0[row_perm])
    bfc = np.asarray(L0_attn_b_fc, dtype=np.float32).reshape(1, D)
    rwT = np.ascontiguousarray(np.asarray(L0_ffn_router_w, dtype=np.float32).T)
    wgT = np.ascontiguousarray(np.asarray(L0_ffn_w_gate, dtype=np.float32).transpose(0, 2, 1))
    wuT = np.ascontiguousarray(np.asarray(L0_ffn_w_up, dtype=np.float32).transpose(0, 2, 1))
    wdT = np.ascontiguousarray(np.asarray(L0_ffn_w_down, dtype=np.float32).transpose(0, 2, 1))
    xf = x.reshape(4 * N, D)
    in_maps = []
    for c in range(NCORES):
        b, hg = c // 2, c % 2
        wq_sl = np.stack([wqkv[:, part * 1024 + hg * 512: part * 1024 + (hg + 1) * 512]
                          for part in range(3)], axis=0)
        wq_r = np.zeros((4, 128, 8, 384), dtype=np.float32)
        for pr in range(4):
            for blk in range(3):
                blkcols = wq_sl[blk][:, 128 * pr:128 * (pr + 1)]
                wq_r[pr, :, :, blk * 128:(blk + 1) * 128] = (
                    blkcols.reshape(8, 128, 128).transpose(1, 0, 2))
        in_maps.append(dict(
            xb=np.ascontiguousarray(x[b]),
            xown=np.ascontiguousarray(xf[TOK * c:TOK * (c + 1)]),
            wqkv=np.ascontiguousarray(wq_r),
            wfc=wfc, bfc=bfc,
            wconv=np.ascontiguousarray(wconv[hg * 8:(hg + 1) * 8]),
            rwT=rwT, wgT=wgT, wuT=wuT, wdT=wdT,
            **CONSTS,
        ))
    return in_maps


_NC_CACHE = {}


def kernel(**inputs):
    in_maps = make_in_maps(**inputs)
    if "nc" not in _NC_CACHE:
        _NC_CACHE["nc"] = build_nc(debug_taps=False)
    nc = _NC_CACHE["nc"]
    res = run_bass_kernel_spmd(nc, in_maps, core_ids=list(range(NCORES)))
    out = np.concatenate([np.asarray(r["y"]) for r in res.results], axis=0)
    return out.reshape(B_, N, D)


# version 20
# speedup vs baseline: 1.0858x; 1.0858x over previous
import numpy as np
import os
import ml_dtypes
import concourse.bass as bass
import concourse.mybir as mybir
from concourse.bass_utils import run_bass_kernel_spmd

F32 = mybir.dt.float32
BF16 = mybir.dt.bfloat16
AF = mybir.ActivationFunctionType
ALU = mybir.AluOpType
AX = mybir.AxisListType
BFD = ml_dtypes.bfloat16

NCORES = 8
B_, N, D = 4, 4096, 1024
NH, DH, NLM, ITERS, KC = 8, 64, 256, 6, 13
TOK = 2048
E, FF = 8, 2048
EPS = 1e-5
QS = 0.125


class Prog:
    ENGS = ("sync", "scalar", "vector", "gpsimd", "tensor")
    DMA_SEMS = ("cst", "xA0", "xA1", "ast", "xm0", "xm1", "fin", "tap",
                "spr", "qw0", "qw1", "gw", "gd", "cc")

    def __init__(self):
        self.ops = {e: [] for e in self.ENGS}
        self.counts = {s: 0 for s in self.ENGS + self.DMA_SEMS}

    def add(self, eng, fn, wait=(), inc_by=0, sem=None):
        tok = None
        sem = sem or eng
        if inc_by:
            self.counts[sem] += inc_by
            tok = (sem, self.counts[sem])
        self.ops[eng].append((fn, [w for w in wait if w is not None], inc_by, sem, eng))
        return tok

    def all_toks(self):
        return [(s, c) for s, c in self.counts.items() if c > 0]

    def emit(self, block, sems):
        def runner(name):
            def _run(engine):
                waited = {}
                for fn, waits, inc_by, sem, eng in self.ops[name]:
                    for (s, v) in waits:
                        if waited.get(s, -1) >= v:
                            continue
                        engine.wait_ge(sems[s], v)
                        waited[s] = v
                    inst = fn(engine)
                    if inc_by:
                        inst.then_inc(sems[sem], inc_by)
            return _run
        block.sync(runner("sync"))
        block.scalar(runner("scalar"))
        block.vector(runner("vector"))
        block.gpsimd(runner("gpsimd"))
        block.tensor(runner("tensor"))


class Bank:
    def __init__(self, ap):
        self.ap = ap
        self.prod = None
        self.cons = None


def build_nc(debug_taps=False):
    nc = bass.Bass("TRN2", target_bir_lowering=False)

    d_x = nc.dram_tensor("xb", [N, D], F32, kind="ExternalInput").ap()
    d_xo = nc.dram_tensor("xown", [TOK, D], F32, kind="ExternalInput").ap()
    d_wqkv = nc.dram_tensor("wqkv", [4, 128, 8, 384], F32, kind="ExternalInput").ap()
    d_wfc = nc.dram_tensor("wfc", [D, D], F32, kind="ExternalInput").ap()
    d_bfc = nc.dram_tensor("bfc", [1, D], F32, kind="ExternalInput").ap()
    d_wconv = nc.dram_tensor("wconv", [NH, KC], F32, kind="ExternalInput").ap()
    d_rwT = nc.dram_tensor("rwT", [D, E], F32, kind="ExternalInput").ap()
    d_wgT = nc.dram_tensor("wgT", [E, D, FF], F32, kind="ExternalInput").ap()
    d_wuT = nc.dram_tensor("wuT", [E, D, FF], F32, kind="ExternalInput").ap()
    d_wdT = nc.dram_tensor("wdT", [E, FF, D], F32, kind="ExternalInput").ap()
    d_eye = nc.dram_tensor("eye_bf", [128, 128], BF16, kind="ExternalInput").ap()
    d_eye7 = nc.dram_tensor("eye7_bf", [128, 128], BF16, kind="ExternalInput").ap()
    d_eye15 = nc.dram_tensor("eye15_bf", [128, 128], BF16, kind="ExternalInput").ap()
    d_eye325 = nc.dram_tensor("eye325_bf", [128, 128], BF16, kind="ExternalInput").ap()
    d_onesc = nc.dram_tensor("ones_col_bf", [128, 1], BF16, kind="ExternalInput").ap()
    d_onesr = nc.dram_tensor("ones_row_bf", [1, 128], BF16, kind="ExternalInput").ap()
    d_onesrf = nc.dram_tensor("ones_row_f", [1, 128], F32, kind="ExternalInput").ap()
    d_pairsel = nc.dram_tensor("pairsel", [2, 128], F32, kind="ExternalInput").ap()

    d_y = nc.dram_tensor("y", [TOK, D], F32, kind="ExternalOutput").ap()

    i_attnT = nc.dram_tensor("i_attnT", [NH * DH, N], BF16, kind="Internal")
    i_pairT = nc.dram_tensor("i_pairT", [2 * NH * DH, N], BF16, kind="Internal")
    i_xmid = nc.dram_tensor("i_xmid", [TOK, D], F32, kind="Internal").ap()

    taps = {}
    if debug_taps:
        taps["t_xnT"] = nc.dram_tensor("t_xnT", [128, 8 * N], BF16, kind="ExternalOutput").ap()
        taps["t_attnT"] = nc.dram_tensor("t_attnT", [NH * DH, N], BF16, kind="ExternalOutput").ap()
        taps["t_xmid"] = nc.dram_tensor("t_xmid", [TOK, D], F32, kind="ExternalOutput").ap()
        taps["t_gates"] = nc.dram_tensor("t_gates", [128, 16 * E], F32, kind="ExternalOutput").ap()
        taps["t_xnlT"] = nc.dram_tensor("t_xnlT", [128, 8 * NLM], BF16, kind="ExternalOutput").ap()
        taps["t_lgt"] = nc.dram_tensor("t_lgt", [128, 16 * E], F32, kind="ExternalOutput").ap()
        taps["t_a2"] = nc.dram_tensor("t_a2", [128, 512], BF16, kind="ExternalOutput").ap()
        taps["t_z"] = nc.dram_tensor("t_z", [128, 512], BF16, kind="ExternalOutput").ap()
        taps["t_o3"] = nc.dram_tensor("t_o3", [128, 128], BF16, kind="ExternalOutput").ap()
        taps["t_rr"] = nc.dram_tensor("t_rr", [128, 128], BF16, kind="ExternalOutput").ap()
        taps["t_e3T"] = nc.dram_tensor("t_e3T", [128, 1024], BF16, kind="ExternalOutput").ap()
        taps["t_v2"] = nc.dram_tensor("t_v2", [128, 2 * TOK], BF16, kind="ExternalOutput").ap()
        taps["t_s3"] = nc.dram_tensor("t_s3", [128, 24], F32, kind="ExternalOutput").ap()

    from contextlib import ExitStack
    ctx = ExitStack()
    with ctx:
        ORIGIN = 19456
        TOP = 229344
        DSZ = {BF16: 2, F32: 4}
        placed = []
        placed_meta = []
        placed_lo = {}

        def place(name, shape, dtype, p0, p1, at=None, at_off=0):
            nbytes = int(np.prod(shape[1:])) * DSZ[dtype]
            nbytes = (nbytes + 31) // 32 * 32
            if at is not None:
                lo = placed_lo[at] + at_off
                placed.append((lo, lo + nbytes, p0, p1))
            else:
                conf = sorted([(lo, hi) for (lo, hi, q0, q1) in placed
                               if not (q1 < p0 or p1 < q0)])
                lo = ORIGIN
                for (clo, chi) in conf:
                    if lo + nbytes <= clo:
                        break
                    lo = max(lo, chi)
                if lo + nbytes > TOP:
                    for nm, l_ in sorted(placed_lo.items(), key=lambda kv: kv[1]):
                        print(f"  {nm:12s} lo={l_}")
                    raise AssertionError(f"sbuf overflow at {name}: {lo}+{nbytes}")
                placed.append((lo, lo + nbytes, p0, p1))
            placed_meta.append((lo, lo + nbytes, p0, p1, name, at))
            placed_lo[name] = lo
            h = nc.alloc_sbuf_tensor_at(name, list(shape), dtype, offset=lo)
            return h.ap()

        def place3(name, outer, inner, dtype, p0, p1, at=None, at_off=0):
            return place(name, [128, outer * inner], dtype, p0, p1, at=at,
                         at_off=at_off).rearrange("p (a b) -> p a b", b=inner)

        cb = place("cb", [128, 128], BF16, 0, 3)
        cbA = place("cbA", [128, 384], BF16, 0, 1)
        eye, eye7 = cb[:, 0:128], cbA[:, 0:128]
        eye15, eye325 = cbA[:, 128:256], cbA[:, 256:384]
        ones_col = place("ones_col", [128, 1], BF16, 0, 3)
        ones_row = place("ones_row", [1, 128], BF16, 0, 3)
        ones_rowf = place("ones_rowf", [1, 128], F32, 0, 3)
        smc = place("smc", [128, 24], F32, 0, 2)
        smcA = place("smcA", [128, 96], F32, 0, 1)
        lncol = [smc[:, 0:8], smc[:, 8:16]]
        m3c = [smcA[:, 0:16], smcA[:, 16:32]]
        hc = [smcA[:, 32:56], smcA[:, 56:80]]
        s1c = [smcA[:, 80:84], smcA[:, 84:88], smcA[:, 88:92], smcA[:, 92:96]]
        epsc = smc[:, 16:17]
        lgt_b = place("lgt", [128, 16], F32, 2, 2)
        lgt2 = [lgt_b[:, 0:8], lgt_b[:, 8:16]]
        sx = place("sx", [128, 192], F32, 2, 2)
        _sxn = ["ind1", "absS", "f1", "d", "t", "u1", "ebuf", "eu", "ms", "ind2", "tmp", "d2"]
        sxv2 = [{k: sx[:, (sl * 12 + i) * 8:(sl * 12 + i + 1) * 8]
                 for i, k in enumerate(_sxn)} for sl in range(2)]
        onebuf = place("onebuf", [1, 16], F32, 1, 1)
        rws = place("rws", [128, 64], F32, 0, 1)
        rw_hi = place3("rw_hi", 8, E, BF16, 0, 2)
        rw_lo = place3("rw_lo", 8, E, BF16, 0, 2)
        wconv_t = place("wconv_t", [2, 4 * KC], F32, 0, 1)
        pairsel_t = place("pairsel_t", [2, 128], F32, 0, 1)
        convS = place("convS", [128, KC], F32, 0, 3)
        gcol = place3("gcol", 16, E, F32, 2, 3)

        xnT = place3("xnT", 8, N, BF16, 1, 1)
        wgt = place3("wgt", 8, FF, BF16, 2, 3, at="xnT")
        wut = place3("wut", 8, FF, BF16, 2, 3, at="xnT", at_off=8 * FF * 2)
        pair_own = place3("pair_own", 8, TOK, BF16, 2, 2)
        xn2T = place3("xn2T", 8, TOK, BF16, 2, 3)
        wfc_bf = place3("wfc_bf", 8, 1024, BF16, 2, 2)
        bfc_t = place("bfc_t", [128, 1024], F32, 2, 2)
        bfc_row = place("bfc_row", [1, 1024], F32, 2, 2)
        xtB = [place(f"xtB{i}", [128, 1024], F32, 2, 2) for i in range(4)]
        smcB = place("smcB", [128, 16], F32, 2, 2)
        xloT_st = place3("xloT_st", 8, 128, BF16, 2, 2)

        wdt = place3("wdt", 16, 1024, BF16, 3, 3)
        hT = place3("hT", 16, 1024, BF16, 3, 3)
        out_acc = place3("out_acc", 8, 1024, F32, 3, 3)
        silu_st = [place("silu0", [128, 512], F32, 3, 3),
                   place("silu1", [128, 512], F32, 3, 3)]
        xtF = place("xtF", [128, 1024], F32, 3, 3)
        sq_f32 = place("sq_f32", [128, 1024], F32, 1, 2)
        taps_lgt_sb = place3("taps_lgt", 16, E, F32, 2, 2) if debug_taps else None

        wq_pp = [place3("wq0", 8, 384, BF16, 1, 1), place3("wq1", 8, 384, BF16, 1, 1)]
        qT2 = place("qT2", [128, N], BF16, 1, 1)
        kT2 = place("kT2", [128, N], BF16, 1, 1)
        vv2v = place("vv2", [128, 2 * 32 * 65], BF16, 1, 1).rearrange(
            "p (j m c) -> p j m c", m=32, c=65)
        sim3 = place3("sim3", 2, N, BF16, 1, 1)
        e3 = place3("e3", 2, N, BF16, 1, 1)
        e3T = place3("e3T", 32, 256, BF16, 1, 1, at="sim3")
        conv_buf = place("conv_buf", [128, 4108], BF16, 1, 1)

        res_pp = [place("res0", [128, TOK], BF16, 1, 2),
                  place("res1", [128, TOK], BF16, 1, 2)]
        xn_st = [place(f"xn_st{i}", [128, 1024], BF16, 1, 2) for i in range(2)]
        mp2 = [{k: place3(f"mp{i}_" + k, 2, 256, BF16, 1, 1)
                for k in ["a", "aT", "zA", "zB", "azTn", "az", "u", "w2", "wT"]}
               for i in range(2)]
        rrb2 = [place3(f"rrb{i}", 2, 64, BF16, 1, 1) for i in range(2)]
        e1T2 = [place3("e1T0", 4, 512, BF16, 1, 1, at="e3"),
                place3("e1T1", 4, 512, BF16, 1, 1, at="e3", at_off=4096)]
        usb_sh = place("usb_sh", [128, 512], BF16, 1, 1, at="e3", at_off=8192)
        ssb_sh = place("ssb_sh", [128, 512], BF16, 1, 1, at="e3", at_off=9216)
        onesp = place("onesp", [128, 64], BF16, 1, 1)
        out3b2 = [place3(f"out3b{i}", 2, 64, BF16, 1, 1) for i in range(2)]
        qlT2 = place("qlT2", [128, 256], BF16, 1, 1)
        klT2 = place("klT2", [128, 256], BF16, 1, 1)
        xnlT = place3("xnlT", 8, 256, BF16, 1, 1)
        sbc = place("sbc", [128, 128], BF16, 1, 1)
        cdiag = place3("cdiag", KC, 128, BF16, 1, 1)
        xtA = [place("xtA0", [128, 1024], F32, 1, 1),
               place("xtA1", [128, 1024], F32, 1, 1),
               place("xtA2", [128, 1024], F32, 1, 1, at="conv_buf", at_off=32),
               place("xtA3", [128, 1024], F32, 1, 1)]
        lnc2 = place("lnc2", [128, 16], F32, 1, 1)
        xnl_f = [place("xnl_f0", [128, 256], F32, 1, 1),
                 place("xnl_f1", [128, 256], F32, 1, 1)]

        peak = max(hi for (lo, hi, _, _) in placed)
        assert peak <= TOP, peak
        if os.environ.get("BASS_LAYOUT"):
            for lo, hi, p0, p1, nm, at in sorted(placed_meta):
                print(f"  {lo:7d}..{hi:7d} ({hi-lo:6d}) p{p0}{p1} {nm}"
                      + (f" at={at}" if at else ""))
            print(f"  peak={peak} TOP={TOP}")

        pA = Bank(ctx.enter_context(nc.psum_tensor("pA", [128, 512], F32)).ap())
        pB = Bank(ctx.enter_context(nc.psum_tensor("pB", [128, 512], F32)).ap())
        pC = Bank(ctx.enter_context(nc.psum_tensor("pC", [128, 512], F32)).ap())
        pD = Bank(ctx.enter_context(nc.psum_tensor("pD", [128, 512], F32)).ap())
        pS = Bank(ctx.enter_context(nc.psum_tensor("pS", [128, 512], F32)).ap())
        pH = Bank(ctx.enter_context(nc.psum_tensor("pH", [128, 512], F32)).ap())
        pT1 = Bank(ctx.enter_context(nc.psum_tensor("pT1", [128, 1024], BF16)).ap())
        pT2 = Bank(ctx.enter_context(nc.psum_tensor("pT2", [128, 1024], BF16)).ap())

        sems = {s: ctx.enter_context(nc.semaphore("sem_" + s))
                for s in Prog.ENGS + Prog.DMA_SEMS}
        P = Prog()

        def dma(eng, sem, out, in_, wait=()):
            return P.add(eng, lambda E_, o=out, i=in_: E_.dma_start(out=o, in_=i),
                         wait=wait, inc_by=16, sem=sem)

        def mm(bank, lhsT, rhs_, start, stop, wait=(), out=None, inc=0):
            o = bank.ap if out is None else out
            w = list(wait)
            if start and bank.cons is not None:
                w.append(bank.cons)
            t = P.add("tensor",
                      lambda E_, o=o, l=lhsT, r=rhs_, s1=start, s2=stop:
                      E_.matmul(o, l, r, start=s1, stop=s2),
                      wait=w, inc_by=inc)
            if inc:
                bank.prod = t
            return t

        def trans(bank, in_, out=None, wait=(), inc=0):
            o = bank.ap if out is None else out
            w = list(wait)
            if bank.cons is not None:
                w.append(bank.cons)
            t = P.add("tensor", lambda E_, o=o, i=in_: E_.transpose(o, i, eye),
                      wait=w, inc_by=inc)
            if inc:
                bank.prod = t
            return t

        def ev(eng, fn, bank=None, wait=(), inc_by=1):
            w = list(wait)
            if bank is not None and bank.prod is not None:
                w.append(bank.prod)
            t = P.add(eng, fn, wait=w, inc_by=inc_by)
            if bank is not None:
                bank.cons = t
            return t

        dma("sync", "cst", eye, d_eye)
        dma("sync", "cst", eye7, d_eye7)
        dma("sync", "cst", eye15, d_eye15)
        dma("sync", "cst", eye325, d_eye325)
        dma("sync", "cst", ones_col[:], d_onesc)
        dma("sync", "cst", ones_row[:], d_onesr)
        dma("sync", "cst", ones_rowf[:], d_onesrf)
        dma("sync", "cst", pairsel_t[:], d_pairsel)
        t_wc = None
        for p in range(4):
            dma("sync", "cst", wconv_t[:, p * KC:(p + 1) * KC],
                d_wconv[2 * p:2 * p + 2, :])
        t_eps = P.add("vector", lambda E_: E_.memset(epsc, EPS), inc_by=1)
        t_cbz = P.add("vector", lambda E_: E_.memset(conv_buf[:, 0:6], 0.0), inc_by=1)
        t_cbz2 = P.add("vector", lambda E_: E_.memset(conv_buf[:, 4102:4108], 0.0),
                       inc_by=1)
        P.add("vector", lambda E_: E_.memset(onesp[0:1, :], 1.0), inc_by=1)
        tok_vones = None
        for jj_ in range(2):
            tok_vones = P.add("vector", lambda E_, jj_=jj_: E_.memset(
                vv2v[:, jj_, :, 64:65], 1.0), inc_by=1)
        tok_rrones = P.add("vector", lambda E_: E_.memset(
            onesp[64:65, :], 1.0), inc_by=1)
        pid_cell = {}
        def _pid_init(E_):
            pid_cell["pid"] = E_.partition_id()
            return E_.nop()
        P.add("scalar", _pid_init, inc_by=0)
        rwsv = rws.rearrange("p (a b) -> p a b", b=E)
        dma("sync", "cst", rwsv, d_rwT.rearrange("(c p) e -> p c e", p=128))
        tok_consts0 = ("cst", P.counts["cst"])
        t_rwh = P.add("vector", lambda E_: E_.tensor_copy(
            rw_hi[:], rwsv), wait=[tok_consts0], inc_by=1)
        t_rwl = P.add("vector", lambda E_: E_.tensor_tensor(
            rw_lo[:], rwsv, rw_hi[:], op=ALU.subtract), wait=[t_rwh], inc_by=1)
        tok_consts = ("cst", P.counts["cst"])

        sq_scr = sq_f32
        t_norm = [None] * 32
        t_sq = [None] * 32
        t_trev = [None] * 32
        lncolA = [lncol[0], lncol[1], lnc2[:, 0:8], lnc2[:, 8:16]]
        for t in range(32):
            xt = xtA[t % 4]
            lc = lncolA[t % 4]
            w = [tok_consts] if t == 0 else []
            if t >= 4:
                w += [t_norm[t - 4], t_sq[t - 4]]
            t_xds = [
                dma("sync", "xA0", xt[:, 0:512],
                    d_x[128 * t:128 * (t + 1), 0:512], wait=w),
                dma("gpsimd", "xA1", xt[:, 512:1024],
                    d_x[128 * t:128 * (t + 1), 512:1024], wait=w),
            ]
            t_xd = t_xds[-1]
            t_sum = P.add("vector", lambda E_, xt=xt, lc=lc: E_.reduce_sum(
                lc[:, 0:1], xt, axis=AX.X), wait=t_xds, inc_by=1)
            t_sq[t] = P.add("scalar", lambda E_, xt=xt, lc=lc: E_.activation(
                sq_scr, xt, AF.Square, accum_out=lc[:, 1:2]),
                wait=t_xds, inc_by=1)
            t_q = P.add("vector", lambda E_, lc=lc: E_.scalar_tensor_tensor(
                lc[:, 2:3], lc[:, 0:1], -1.0 / D, lc[:, 0:1], op0=ALU.mult, op1=ALU.mult),
                wait=[t_sum], inc_by=1)
            t_var = P.add("vector", lambda E_, lc=lc: E_.tensor_tensor(
                lc[:, 4:5], lc[:, 1:2], lc[:, 2:3], op=ALU.add),
                wait=[t_q, t_sq[t]], inc_by=1)
            t_ln = P.add("scalar", lambda E_, lc=lc: E_.activation(
                lc[:, 5:6], lc[:, 4:5], AF.Ln, bias=epsc, scale=1.0 / D),
                wait=[t_var, t_eps], inc_by=1)
            t_rs = P.add("scalar", lambda E_, lc=lc: E_.activation(
                lc[:, 6:7], lc[:, 5:6], AF.Exp, scale=-0.5), wait=[t_ln], inc_by=1)
            t_nb = P.add("vector", lambda E_, lc=lc: E_.scalar_tensor_tensor(
                lc[:, 7:8], lc[:, 0:1], -1.0 / D, lc[:, 6:7], op0=ALU.mult, op1=ALU.mult),
                wait=[t_rs], inc_by=1)
            xs = xn_st[t % 2]
            wnorm = [t_nb] + ([t_trev[t - 2]] if t >= 2 else [])
            t_norm[t] = P.add("vector", lambda E_, xt=xt, lc=lc, xs=xs: E_.tensor_scalar(
                xs, xt, lc[:, 6:7], lc[:, 7:8], op0=ALU.mult, op1=ALU.add),
                wait=wnorm, inc_by=1)
            bankT = pT1 if t % 2 == 0 else pT2
            for kc in range(8):
                trans(bankT, xs[0:128, kc * 128:(kc + 1) * 128],
                      out=bankT.ap[:, kc * 128:(kc + 1) * 128],
                      wait=[t_norm[t]] if kc == 0 else (), inc=(kc == 7))
            if t % 2 == 0:
                t_trev[t] = ev("scalar", lambda E_, t=t, bankT=bankT: E_.copy(
                    xnT[:, :, 128 * t:128 * (t + 1)],
                    bankT.ap.rearrange("p (a b) -> p a b", b=128)), bank=bankT)
            else:
                t_trev[t] = ev("vector", lambda E_, t=t, bankT=bankT:
                               E_.tensor_copy(
                                   xnT[:, :, 128 * t:128 * (t + 1)],
                                   bankT.ap.rearrange("p (a b) -> p a b", b=128)),
                               bank=bankT)
        tok_xnT = t_trev[31]
        tok_ln_last = (t_norm[31], t_sq[31])

        t_xnl = [None, None]
        for kc in range(8):
            xf = xnl_f[kc % 2]
            t_r = P.add("vector", lambda E_, kc=kc, xf=xf: E_.tensor_reduce(
                xf, xnT[:, kc, :].rearrange("p (m w) -> p m w", w=16),
                AX.X, ALU.add), wait=[tok_xnT, t_xnl[kc % 2]], inc_by=1)
            t_xnl[kc % 2] = P.add("gpsimd" if kc % 2 == 0 else "vector",
                lambda E_, kc=kc, xf=xf: E_.tensor_scalar(
                    xnlT[:, kc, :], xf, 1.0 / 16.0, None, op0=ALU.mult),
                wait=[t_r], inc_by=1)
        tok_xnl = t_xnl[1]

        if debug_taps:
            dma("sync", "tap", taps["t_xnT"], xnT.rearrange("p a b -> p (a b)"),
                wait=[tok_xnT])
            dma("sync", "tap", taps["t_xnlT"], xnlT.rearrange("p a b -> p (a b)"),
                wait=[tok_xnl])

        t_wq_lastread = [None, None]
        t_attn_dma = []
        t_res_free = [None, None]
        tok_a_pe_end = None
        cdiag_free = None
        s1_e1_free = [[None] * 4, [None] * 4]
        s1_s_free = [[None] * 2, [None] * 2]
        s1_u_free = [[None] * 2, [None] * 2]

        def emit_interleaved(gens):
            gens = [g for g in gens]
            while gens:
                nxt = []
                for g in gens:
                    try:
                        next(g)
                        nxt.append(g)
                    except StopIteration:
                        pass
                gens = nxt

        for p in range(4):
            pp = p % 2
            pair_ew_bar = [("scalar", P.counts["scalar"]),
                           ("gpsimd", P.counts["gpsimd"])]
            wq = wq_pp[pp]
            wload = [t_wq_lastread[pp], tok_consts]
            t_wq = dma("gpsimd", "qw0" if pp == 0 else "qw1",
                       wq, d_wqkv[p], wait=wload)
            t_qk_ev = None
            for dst, cofs, qs in [(qT2, 0, QS), (kT2, 128, None)]:
                for n in range(8):
                    bank = pA if n % 2 == 0 else pB
                    for kc in range(8):
                        mm(bank, wq[:, kc, cofs:cofs + 128],
                           xnT[:, kc, n * 512:(n + 1) * 512],
                           start=(kc == 0), stop=(kc == 7),
                           wait=[t_wq, t_trev[4 * n + 3]] if kc == 0 else (),
                           inc=(kc == 7))
                    if qs is None:
                        t_qk_ev = ev("vector", lambda E_, dst=dst, n=n, bank=bank:
                                     E_.tensor_copy(dst[:, n * 512:(n + 1) * 512],
                                                    bank.ap), bank=bank)
                    else:
                        t_qk_ev = ev("vector", lambda E_, dst=dst, n=n, bank=bank, qs=qs:
                                     E_.tensor_scalar(dst[:, n * 512:(n + 1) * 512],
                                                      bank.ap, qs, None, op0=ALU.mult),
                                     bank=bank)
            t_v_ev = None
            for j in range(2):
                for mg in range(4):
                    bank = pC if mg % 2 == 0 else pD
                    for ms in range(8):
                        mchunk = mg * 8 + ms
                        for kc in range(8):
                            mm(bank, xnT[:, kc, mchunk * 128:(mchunk + 1) * 128],
                               wq[:, kc, 256 + j * 64:256 + j * 64 + 64],
                               start=(kc == 0), stop=(kc == 7),
                               out=bank.ap[:, ms * 64:(ms + 1) * 64],
                               wait=[t_wq, t_trev[mg * 8 + 7]] if (ms == 0 and kc == 0) else (),
                               inc=(kc == 7 and ms == 7))
                    t_v_ev = ev("vector", lambda E_, j=j, mg=mg, bank=bank:
                                E_.tensor_copy(
                                    vv2v[:, j, mg * 8:(mg + 1) * 8, 0:64],
                                    bank.ap.rearrange("p (a b) -> p a b", b=64)),
                                bank=bank)
            for dst, cofs, qs in [(qlT2, 0, QS), (klT2, 128, None)]:
                for kc in range(8):
                    mm(pS, wq[:, kc, cofs:cofs + 128], xnlT[:, kc, :],
                       start=(kc == 0), stop=(kc == 7), out=pS.ap[:, 0:256],
                       wait=[tok_xnl] if kc == 0 else (), inc=(kc == 7))
                if qs is None:
                    ev("vector", lambda E_, dst=dst: E_.tensor_copy(
                        dst[:], pS.ap[:, 0:256]), bank=pS)
                else:
                    ev("vector", lambda E_, dst=dst, qs=qs: E_.tensor_scalar(
                        dst[:], pS.ap[:, 0:256], qs, None, op0=ALU.mult), bank=pS)
            t_lmk_ev = ("vector", P.counts["vector"])
            t_cs = mm(pS, pairsel_t[:], wconv_t[:, p * KC:(p + 1) * KC], True, True,
                      wait=[tok_consts, t_res_free[0]], out=pS.ap[:, 0:KC], inc=1)
            t_csv = ev("vector", lambda E_: E_.tensor_copy(convS[:], pS.ap[:, 0:KC]),
                       bank=pS, wait=pair_ew_bar + [t_res_free[0]])
            t_wq_lastread[pp] = ("tensor", P.counts["tensor"])

            HS = [dict(), dict()]
            for j in range(2):
                HS[j]["C"] = hc[j]
                HS[j]["m3c"] = m3c[j]
                HS[j]["mp"] = mp2[j]
                HS[j]["out3b"] = out3b2[j]
                HS[j]["rrb"] = rrb2[j]
                HS[j]["ql"] = qlT2[64 * j:64 * j + 64, :]
                HS[j]["kl"] = klT2[64 * j:64 * j + 64, :]
                HS[j]["qTh"] = qT2[64 * j:64 * j + 64, :]
                HS[j]["kTh"] = kT2[64 * j:64 * j + 64, :]

            def sim2_z0(j, bar):
                h = HS[j]
                C, mpb = h["C"], h["mp"]
                ql, kl = h["ql"], h["kl"]
                for mc in range(2):
                    mm(pS, ql[:, mc * 128:(mc + 1) * 128], kl, True, True,
                       out=pS.ap[:, mc * 256:(mc + 1) * 256],
                       wait=bar if mc == 0 else (), inc=(mc == 1))
                t_aev = None
                for mc in range(2):
                    t_mx = P.add("vector", lambda E_, mc=mc, C=C: E_.reduce_max(
                        C[:, 0 + mc:1 + mc], pS.ap[:, mc * 256:(mc + 1) * 256],
                        axis=AX.X, negate=True), wait=[pS.prod], inc_by=1)
                    t_aev = P.add("scalar", lambda E_, mc=mc, C=C: E_.activation(
                        mpb["a"][:, mc, :], pS.ap[:, mc * 256:(mc + 1) * 256], AF.Exp,
                        bias=C[:, 0 + mc:1 + mc], accum_out=C[:, 4 + mc:5 + mc]),
                        wait=[t_mx], inc_by=1)
                pS.cons = t_aev
                t_rc = P.add("vector", lambda E_, C=C: E_.reciprocal(
                    C[:, 6:8], C[:, 4:6]), wait=[t_aev], inc_by=1)
                t_an = None
                for mc in range(2):
                    t_an = P.add("scalar", lambda E_, mc=mc, C=C, mpb=mpb: E_.activation(
                        mpb["a"][:, mc, :], mpb["a"][:, mc, :], AF.Copy,
                        scale=C[:, 6 + mc:7 + mc]), wait=[t_rc], inc_by=1)
                bankT = pT1
                for kc in range(2):
                    for jj in range(2):
                        trans(bankT, mpb["a"][:, jj, kc * 128:(kc + 1) * 128],
                              out=bankT.ap[:, (kc * 2 + jj) * 128:(kc * 2 + jj + 1) * 128],
                              wait=[t_an] if (kc == 0 and jj == 0) else (),
                              inc=(kc == 1 and jj == 1))
                t_aT = ev("scalar", lambda E_, mpb=mpb: E_.copy(
                    mpb["aT"].rearrange("p a b -> p (a b)"), bankT.ap[:, 0:512]),
                    bank=bankT)
                h["t_aT"] = t_aT
                h["t_an"] = t_an
                for jj in range(2):
                    mm(pH, ones_col[:], mpb["a"][:, jj, :], start=(jj == 0), stop=(jj == 1),
                       out=pH.ap[0:1, 0:256], wait=[t_an] if jj == 0 else (), inc=(jj == 1))
                for kc in range(2):
                    mm(pH, ones_col[:], mpb["aT"][:, kc, :], start=(kc == 0), stop=(kc == 1),
                       out=pH.ap[0:1, 256:512], wait=[t_aT] if kc == 0 else (), inc=(kc == 1))
                t_c1 = P.add("vector", lambda E_: E_.reduce_max(
                    onebuf[0:1, 0:1], pH.ap[0:1, 0:256], axis=AX.X),
                    wait=[pH.prod], inc_by=1)
                t_c2 = P.add("vector", lambda E_: E_.reduce_max(
                    onebuf[0:1, 1:2], pH.ap[0:1, 256:512], axis=AX.X),
                    wait=[pH.prod], inc_by=1)
                pH.cons = t_c2
                t_c3 = P.add("vector", lambda E_: E_.tensor_tensor(
                    onebuf[0:1, 2:3], onebuf[0:1, 0:1], onebuf[0:1, 1:2], op=ALU.mult),
                    wait=[t_c1, t_c2], inc_by=1)
                t_c4 = P.add("vector", lambda E_: E_.reciprocal(
                    onebuf[0:1, 3:4], onebuf[0:1, 2:3]), wait=[t_c3], inc_by=1)
                t_c5 = P.add("vector", lambda E_: E_.tensor_copy(
                    sbc[0:1, 0:1], onebuf[0:1, 3:4]), wait=[t_c4], inc_by=1)
                t_bc = mm(pH, ones_row[:], sbc[0:1, 0:1], True, True,
                          out=pH.ap[:, 320:321], wait=[t_c5], inc=1)
                t_scol = ev("vector", lambda E_, C=C: E_.tensor_copy(
                    C[:, 8:9], pH.ap[:, 320:321]), bank=pH)
                h["t_z"] = P.add("scalar", lambda E_, C=C, mpb=mpb: E_.activation(
                    mpb["zA"].rearrange("p a b -> p (a b)"),
                    mpb["aT"].rearrange("p a b -> p (a b)"), AF.Copy,
                    scale=C[:, 8:9]), wait=[t_scol, t_aT], inc_by=1)

            def mp_iters(j, bM, bU, bT_):
                h = HS[j]
                mpb = h["mp"]
                t_z = h["t_z"]
                t_aT = h["t_aT"]
                zin, zout = "zA", "zB"
                for it in range(ITERS):
                    for mc in range(2):
                        for kc in range(2):
                            mm(bM, mpb["aT"][:, kc, mc * 128:(mc + 1) * 128],
                               mpb[zin][:, kc, :], start=(kc == 0), stop=(kc == 1),
                               out=bM.ap[:, mc * 256:(mc + 1) * 256],
                               wait=[t_z, t_aT] if (mc == 0 and kc == 0) else (),
                               inc=(mc == 1 and kc == 1))
                    t_az = ev("scalar", lambda E_, mpb=mpb: E_.copy(
                        mpb["az"].rearrange("p a b -> p (a b)"), bM.ap[:]), bank=bM)
                    yield
                    for kc in range(2):
                        for jj in range(2):
                            trans(bT_, mpb["az"][:, jj, kc * 128:(kc + 1) * 128],
                                  out=bT_.ap[:, (kc * 2 + jj) * 128:(kc * 2 + jj + 1) * 128],
                                  wait=[t_az] if (kc == 0 and jj == 0) else (),
                                  inc=(kc == 1 and jj == 1))
                    t_azT = ev("scalar", lambda E_, mpb=mpb: E_.activation(
                        mpb["azTn"].rearrange("p a b -> p (a b)"), bT_.ap[:, 0:512],
                        AF.Copy, scale=-1.0), bank=bT_)
                    yield
                    for mc in range(2):
                        mm(bU, eye7, mpb["az"][:, mc, :], start=True, stop=False,
                           out=bU.ap[:, mc * 256:(mc + 1) * 256],
                           wait=[t_az, t_azT] if mc == 0 else ())
                        for kc in range(2):
                            mm(bU, mpb["azTn"][:, kc, mc * 128:(mc + 1) * 128],
                               mpb["az"][:, kc, :], start=False, stop=(kc == 1),
                               out=bU.ap[:, mc * 256:(mc + 1) * 256],
                               inc=(mc == 1 and kc == 1))
                    t_u = ev("vector", lambda E_, mpb=mpb: E_.tensor_copy(
                        mpb["u"].rearrange("p a b -> p (a b)"), bU.ap[:]), bank=bU)
                    yield
                    for mc in range(2):
                        mm(bM, eye15, mpb["az"][:, mc, :], start=True, stop=False,
                           out=bM.ap[:, mc * 256:(mc + 1) * 256],
                           wait=[t_u] if mc == 0 else ())
                        for kc in range(2):
                            mm(bM, mpb["azTn"][:, kc, mc * 128:(mc + 1) * 128],
                               mpb["u"][:, kc, :], start=False, stop=(kc == 1),
                               out=bM.ap[:, mc * 256:(mc + 1) * 256],
                               inc=(mc == 1 and kc == 1))
                    t_w2 = ev("scalar", lambda E_, mpb=mpb: E_.copy(
                        mpb["w2"].rearrange("p a b -> p (a b)"), bM.ap[:]), bank=bM)
                    yield
                    for kc in range(2):
                        for jj in range(2):
                            trans(bT_, mpb[zin][:, jj, kc * 128:(kc + 1) * 128],
                                  out=bT_.ap[:, (kc * 2 + jj) * 128:(kc * 2 + jj + 1) * 128],
                                  wait=[t_w2] if (kc == 0 and jj == 0) else (),
                                  inc=(kc == 1 and jj == 1))
                    t_zT = ev("scalar", lambda E_, mpb=mpb: E_.activation(
                        mpb["azTn"].rearrange("p a b -> p (a b)"), bT_.ap[:, 0:512],
                        AF.Copy, scale=-0.25), bank=bT_)
                    yield
                    for mc in range(2):
                        mm(bU, eye325, mpb[zin][:, mc, :], start=True, stop=False,
                           out=bU.ap[:, mc * 256:(mc + 1) * 256],
                           wait=[t_zT] if mc == 0 else ())
                        for kc in range(2):
                            mm(bU, mpb["azTn"][:, kc, mc * 128:(mc + 1) * 128],
                               mpb["w2"][:, kc, :], start=False, stop=(kc == 1),
                               out=bU.ap[:, mc * 256:(mc + 1) * 256],
                               inc=(mc == 1 and kc == 1))
                    t_z = ev("vector", lambda E_, mpb=mpb, zout=zout: E_.tensor_copy(
                        mpb[zout].rearrange("p a b -> p (a b)"), bU.ap[:]), bank=bU)
                    zin, zout = zout, zin
                    yield
                for kc in range(2):
                    for jj in range(2):
                        trans(bT_, mpb[zin][:, jj, kc * 128:(kc + 1) * 128],
                              out=bT_.ap[:, (kc * 2 + jj) * 128:(kc * 2 + jj + 1) * 128],
                              wait=[t_z] if (kc == 0 and jj == 0) else (),
                              inc=(kc == 1 and jj == 1))
                h["t_wT"] = ev("scalar", lambda E_, mpb=mpb: E_.copy(
                    mpb["wT"].rearrange("p a b -> p (a b)"), bT_.ap[:, 0:512]), bank=bT_)

            def sim3_out3(j, b1, b2, o3a, o3b_):
                h = HS[j]
                C = h["C"]
                ql, kTh = h["ql"], h["kTh"]
                t_exs = []
                for tc in range(32):
                    bank = b1 if (tc // 2) % 2 == 0 else b2
                    half = tc % 2
                    mm(bank, kTh[:, tc * 128:(tc + 1) * 128], ql, True, True,
                       out=bank.ap[:, half * 256:(half + 1) * 256],
                       wait=[t_lmk_ev] if tc == 0 else (), inc=1)
                    t_exs.append(ev(
                        "scalar", lambda E_, tc=tc, bank=bank, half=half:
                        E_.activation(e3T[:, tc, :],
                                      bank.ap[:, half * 256:(half + 1) * 256],
                                      AF.Exp), bank=bank))
                    if tc % 4 == 3:
                        yield
                o3bank = [o3a, o3b_]
                for m in range(32):
                    for mc in range(2):
                        mm(o3bank[mc], e3T[:, m, mc * 128:(mc + 1) * 128],
                           vv2v[:, j, m, 0:65],
                           start=(m == 0), stop=(m == 31),
                           out=o3bank[mc].ap[:, 0:65],
                           wait=[t_exs[m]] +
                           ([t_v_ev, tok_vones] if (m == 0 and mc == 0) else []),
                           inc=(m == 31))
                    if m % 8 == 7:
                        yield
                t_o3 = None
                for mc in range(2):
                    t_rc3 = P.add("vector", lambda E_, mc=mc, C=C, bk=o3bank[mc]:
                                  E_.reciprocal(C[:, 13 + mc:14 + mc],
                                                bk.ap[:, 64:65]),
                                  wait=[o3bank[mc].prod], inc_by=1)
                    t_o3 = ev("scalar", lambda E_, mc=mc, C=C, j=j, bk=o3bank[mc]:
                              E_.activation(HS[j]["out3b"][:, mc, :],
                                            bk.ap[:, 0:64], AF.Copy,
                                            scale=C[:, 13 + mc:14 + mc]),
                              bank=o3bank[mc], wait=[t_rc3])
                h["t_o3"] = t_o3

            def rr_sim1(j, b1, b2, t_conv_prev):
                h = HS[j]
                mpb, rrb_ = h["mp"], h["rrb"]
                qTh, kl = h["qTh"], h["kl"]
                e1t = e1T2[j]
                P0 = 64 * j
                PS_ = 64 - P0
                bO = [pH, pS]
                for mc in range(2):
                    for kc in range(2):
                        mm(pS, mpb["wT"][:, kc, mc * 128:(mc + 1) * 128],
                           HS[j]["out3b"][:, kc, :], start=(kc == 0), stop=(kc == 1),
                           out=pS.ap[:, mc * 64:(mc + 1) * 64],
                           wait=[h["t_wT"], h["t_o3"]] if (mc == 0 and kc == 0) else (),
                           inc=(mc == 1 and kc == 1))
                t_rr = ev("vector", lambda E_, rrb_=rrb_: E_.tensor_copy(
                    rrb_.rearrange("p a b -> p (a b)"),
                    pS.ap[:, 0:128]), bank=pS)
                yield
                for g in range(8):
                    sl = g % 2
                    t_e = [None, None]
                    for mc, bank in ((0, b1), (1, b2)):
                        mm(bank, kl[:, mc * 128:(mc + 1) * 128],
                           qTh[:, g * 512:(g + 1) * 512], start=True, stop=True,
                           wait=[t_lmk_ev] if (g == 0 and mc == 0) else (), inc=1)
                        t_e[mc] = ev("scalar", lambda E_, sl=sl, mc=mc, bank=bank,
                                     e1t=e1t: E_.activation(
                                         e1t[:, sl * 2 + mc, :], bank.ap, AF.Exp),
                                     bank=bank, wait=[s1_e1_free[j][sl * 2 + mc]])
                    bo = bO[sl]
                    for kc in range(2):
                        mm(bo, rrb_[:, kc, :], e1t[:, sl * 2 + kc, :],
                           start=(kc == 0), stop=(kc == 1),
                           out=bo.ap[P0:P0 + 64, 0:512],
                           wait=([t_rr, t_e[0], t_e[1], tok_rrones]
                                 if kc == 0 else ()),
                           inc=(kc == 1))
                    for kc in range(2):
                        mm(bo, ones_col[:], e1t[:, sl * 2 + kc, :],
                           start=(kc == 0), stop=(kc == 1),
                           out=bo.ap[PS_:PS_ + 1, 0:512],
                           inc=(kc == 1))
                    s1_e1_free[j][sl * 2 + 0] = bo.prod
                    s1_e1_free[j][sl * 2 + 1] = bo.prod
                    def _recip_bf(E_, bo=bo, PS_=PS_):
                        with nc.allow_low_precision(
                                reason="attn1 normalizer row in bf16, matches "
                                       "bf16 storage of normalized attn"):
                            return E_.reciprocal(ssb_sh[PS_:PS_ + 1, 0:512],
                                                 bo.ap[PS_:PS_ + 1, 0:512])
                    t_rc1 = P.add("vector", _recip_bf,
                                  wait=[bo.prod, s1_s_free[j][0]], inc_by=1)
                    t_u = P.add("vector", lambda E_, bo=bo, P0=P0:
                                E_.tensor_copy(usb_sh[P0:P0 + 64, 0:512],
                                               bo.ap[P0:P0 + 64, 0:512]),
                                wait=[bo.prod], inc_by=1)
                    bo.cons = t_u
                    bb = bO[1 - sl]
                    t_bc = mm(bb, onesp[PS_:PS_ + 1, :],
                              ssb_sh[PS_:PS_ + 1, 0:512], True, True,
                              out=bb.ap[P0:P0 + 64, 0:512], wait=[t_rc1], inc=1)
                    s1_s_free[j][0] = t_bc
                    t_mul = ev("vector", lambda E_, g=g, j=j, bb=bb, P0=P0:
                               E_.tensor_tensor(
                                   conv_buf[P0:P0 + 64,
                                            6 + g * 512:6 + (g + 1) * 512],
                                   bb.ap[P0:P0 + 64, 0:512],
                                   usb_sh[P0:P0 + 64, 0:512],
                                   op=ALU.mult), bank=bb,
                               wait=[t_u] + ([t_conv_prev] if g == 0 else []))
                    if g == 7:
                        h["t_oh"] = t_mul
                    yield

            sim2_z0(0, pair_ew_bar)
            emit_interleaved([sim3_out3(0, pA, pB, pS, pH)])
            sim2_z0(1, [])
            emit_interleaved([
                mp_iters(0, pC, pD, pT2),
                sim3_out3(1, pA, pB, pS, pH),
            ])
            emit_interleaved([
                mp_iters(1, pC, pD, pT2),
                rr_sim1(0, pA, pB, t_res_free[0]),
            ])
            emit_interleaved([rr_sim1(1, pC, pD, t_res_free[0])])
            tok_a_pe_end = ("tensor", P.counts["tensor"])
            t_oh_ev = HS[1]["t_oh"]
            t_oh_ev0 = HS[0]["t_oh"]

            t_oh_all = [t_oh_ev, t_oh_ev0]
            t_diag = None
            for tap in range(KC):
                t_diag = P.add("vector", lambda E_, tap=tap: E_.tensor_scalar(
                    cdiag[:, tap, :], eye, convS[:, tap:tap + 1], None, op0=ALU.mult),
                    wait=[t_csv, tok_consts, (cdiag_free if p > 0 else None)], inc_by=1)
            for hf in range(2):
                base = hf * 2048
                for nn in range(4):
                    bank = pC if nn % 2 == 0 else pD
                    for tap in range(KC):
                        mm(bank, cdiag[:, tap, :],
                           conv_buf[:, base + nn * 512 + tap:base + nn * 512 + tap + 512],
                           start=(tap == 0), stop=(tap == KC - 1),
                           wait=(t_oh_all + [t_diag, t_cbz, t_cbz2,
                                             t_res_free[0], t_res_free[1]])
                           if (hf == 0 and nn == 0 and tap == 0) else (),
                           inc=(tap == KC - 1))
                    ev("vector", lambda E_, base=base, nn=nn, bank=bank:
                       E_.tensor_tensor(res_pp[0][0:128, nn * 512:(nn + 1) * 512],
                                        bank.ap,
                                        conv_buf[:, 6 + base + nn * 512:
                                                 6 + base + (nn + 1) * 512],
                                        op=ALU.add), bank=bank,
                       wait=[t_res_free[1]] if nn == 0 else ())
                t_fin = ("vector", P.counts["vector"])
                t_st = dma("sync", "ast",
                           i_attnT.ap()[128 * p:128 * (p + 1), base:base + 2048],
                           res_pp[0], wait=[t_fin])
                t_attn_dma.append(t_st)
                t_res_free = [t_fin, t_st]
            cdiag_free = ("tensor", P.counts["tensor"])

        t_ag = None
        for p4 in range(4):
            t_ag = P.add("gpsimd", lambda E_, p4=p4: E_.collective_compute(
                "AllGather", ALU.bypass,
                replica_groups=[[0, 1], [2, 3], [4, 5], [6, 7]],
                ins=[i_attnT.ap()[128 * p4:128 * (p4 + 1), :].opt()],
                outs=[i_pairT.ap()[256 * p4:256 * (p4 + 1), :].opt()]),
                wait=t_attn_dma[2 * p4:2 * p4 + 2], inc_by=1, sem="cc")
        if debug_taps:
            dma("sync", "tap", taps["t_attnT"], i_attnT.ap(), wait=t_attn_dma)

        def _load_pair(E_, kc):
            pid = pid_cell["pid"]
            return E_.dma_start(out=pair_own[:, kc, :],
                                in_=i_pairT.ap()[128 * kc:128 * (kc + 1),
                                                 bass.ts(pid % 2, TOK)])
        tok_pair = None
        t_po = []
        for kc in range(8):
            tok_pair = P.add("scalar", lambda E_, kc=kc: _load_pair(E_, kc),
                             wait=[("cc", kc // 2 + 1), tok_a_pe_end,
                                   ("vector", P.counts["vector"]),
                                   ("gpsimd", P.counts["gpsimd"])],
                             inc_by=16, sem="spr")
            t_po.append(tok_pair)

        BAR_A = P.all_toks()
        t_wfc = dma("gpsimd", "gw", wfc_bf,
                    d_wfc.rearrange("(c p) n -> p c n", p=128), wait=BAR_A)
        t_bfcr = dma("sync", "cst", bfc_row[:], d_bfc, wait=BAR_A)
        t_bfc = None
        for nn in range(2):
            bank = pA if nn == 0 else pB
            mm(bank, ones_rowf[:], bfc_row[0:1, nn * 512:(nn + 1) * 512], True, True,
               wait=[t_bfcr, t_rwl] if nn == 0 else (), inc=1)
            t_bfc = ev("vector", lambda E_, nn=nn, bank=bank: E_.tensor_copy(
                bfc_t[:, nn * 512:(nn + 1) * 512], bank.ap), bank=bank)

        pre_wg = dma("gpsimd", "gw", wgt,
                     d_wgT[0].rearrange("(c p) n -> p c n", p=128), wait=BAR_A)
        pre_wu = dma("gpsimd", "gw", wut,
                     d_wuT[0].rearrange("(c p) n -> p c n", p=128), wait=BAR_A)
        t_xn2ev = [None] * 16
        t_gates = [None] * 16
        t_xmid_dma = [None] * 16
        lncolB = [smc[:, 0:8], smc[:, 8:16], smcB[:, 0:8], smcB[:, 8:16]]
        prev_lo = [None] * 4
        prev_sq2 = [None] * 4
        prev_loev = [None, None]
        prev_gates = None

        def phaseB_m(m):
            xt = xtB[m % 4]
            lc = lncolB[m % 4]
            xm = xt
            w = list(BAR_A) if m < 4 else [prev_lo[m % 4], prev_sq2[m % 4]]
            t_xt = dma("sync", "xA0" if m % 2 == 0 else "xA1",
                       xt, d_xo[128 * m:128 * (m + 1), :], wait=w)
            fcb = [None, None]
            fcpair = (pA, pB) if m % 2 == 0 else (pC, pD)
            for nn in range(2):
                bank = fcpair[nn]
                for kc in range(8):
                    mm(bank, pair_own[:, kc, m * 128:(m + 1) * 128],
                       wfc_bf[:, kc, nn * 512:(nn + 1) * 512],
                       start=(kc == 0), stop=(kc == 7),
                       wait=[t_po[kc], t_wfc] if m == 0 else (),
                       inc=(kc == 7))
                fcb[nn] = bank
            t_xb = P.add("gpsimd", lambda E_, xt=xt: E_.tensor_tensor(
                xt, bfc_t, xt, op=ALU.add), wait=[t_xt, t_bfc], inc_by=1)
            t_x1 = None
            for nn in range(2):
                bank = fcb[nn]
                t_x1 = ev("vector", lambda E_, nn=nn, bank=bank, xt=xt:
                          E_.scalar_tensor_tensor(
                              xt[:, nn * 512:(nn + 1) * 512], bank.ap, 1.0,
                              xt[:, nn * 512:(nn + 1) * 512],
                              op0=ALU.mult, op1=ALU.add),
                          bank=bank, wait=[t_xb])
            t_x2 = t_x1
            t_xmid_dma[m] = dma("sync", "xm0" if m % 2 == 0 else "xm1",
                                i_xmid[128 * m:128 * (m + 1), :], xm, wait=[t_x2])
            t_xmtap = None
            if debug_taps:
                t_xmtap = dma("sync", "tap", taps["t_xmid"][128 * m:128 * (m + 1), :],
                              xm, wait=[t_x2])
            t_sum = P.add("vector", lambda E_, xm=xm, lc=lc: E_.reduce_sum(
                lc[:, 0:1], xm, axis=AX.X), wait=[t_x2], inc_by=1)
            t_sq2 = P.add("scalar", lambda E_, xm=xm, lc=lc: E_.activation(
                sq_f32, xm, AF.Square, accum_out=lc[:, 1:2]), wait=[t_x2], inc_by=1)
            prev_sq2[m % 4] = t_sq2
            t_q = P.add("vector", lambda E_, lc=lc: E_.scalar_tensor_tensor(
                lc[:, 2:3], lc[:, 0:1], -1.0 / D, lc[:, 0:1], op0=ALU.mult, op1=ALU.mult),
                wait=[t_sum], inc_by=1)
            t_var = P.add("vector", lambda E_, lc=lc: E_.tensor_tensor(
                lc[:, 4:5], lc[:, 1:2], lc[:, 2:3], op=ALU.add),
                wait=[t_q, t_sq2], inc_by=1)
            t_ln = P.add("scalar", lambda E_, lc=lc: E_.activation(
                lc[:, 5:6], lc[:, 4:5], AF.Ln, bias=epsc, scale=1.0 / D),
                wait=[t_var], inc_by=1)
            t_rs = P.add("scalar", lambda E_, lc=lc: E_.activation(
                lc[:, 6:7], lc[:, 5:6], AF.Exp, scale=-0.5), wait=[t_ln], inc_by=1)
            t_nb = P.add("vector", lambda E_, lc=lc: E_.scalar_tensor_tensor(
                lc[:, 7:8], lc[:, 0:1], -1.0 / D, lc[:, 6:7], op0=ALU.mult, op1=ALU.mult),
                wait=[t_rs], inc_by=1)
            xb2 = xn_st[m % 2]
            t_nbf = P.add("gpsimd", lambda E_, xm=xm, lc=lc, xb2=xb2: E_.tensor_scalar(
                xb2, xm, lc[:, 6:7], lc[:, 7:8], op0=ALU.mult, op1=ALU.add),
                wait=[t_nb, t_xn2ev[m - 2] if m >= 2 else None], inc_by=1)
            t_nf = P.add("scalar", lambda E_, xm=xm, lc=lc, xt=xt: E_.activation(
                xt, xm, AF.Identity, scale=lc[:, 6:7], bias=lc[:, 7:8]),
                wait=[t_nb, t_nbf, t_x1, t_xmid_dma[m], t_xmtap], inc_by=1)
            lo_t = res_pp[m % 2]
            t_lo = P.add("gpsimd", lambda E_, xt=xt, xb2=xb2, lo_t=lo_t: E_.tensor_tensor(
                lo_t[0:128, 0:1024], xt, xb2, op=ALU.subtract),
                wait=[t_nf, t_nbf, prev_loev[m % 2]] +
                     (t_res_free if m < 2 else []), inc_by=1)
            prev_lo[m % 4] = t_lo
            for kc in range(8):
                trans(pT1, xb2[0:128, kc * 128:(kc + 1) * 128],
                      out=pT1.ap[:, kc * 128:(kc + 1) * 128],
                      wait=[t_nbf] if kc == 0 else (), inc=(kc == 7))
            t_xn2ev[m] = ev("scalar", lambda E_, m=m: E_.copy(
                xn2T[:, :, m * 128:(m + 1) * 128],
                pT1.ap.rearrange("p (a b) -> p a b", b=128)), bank=pT1)
            for kc in range(8):
                trans(pT2, lo_t[0:128, kc * 128:(kc + 1) * 128],
                      out=pT2.ap[:, kc * 128:(kc + 1) * 128],
                      wait=[t_lo] if kc == 0 else (), inc=(kc == 7))
            t_loev = ev("scalar", lambda E_: E_.copy(
                xloT_st.rearrange("p a b -> p (a b)"), pT2.ap[:]), bank=pT2)
            prev_loev[m % 2] = t_loev
            nmm = 0
            for w_ in (rw_hi, rw_lo):
                for kc in range(8):
                    mm(pS, xn2T[:, kc, m * 128:(m + 1) * 128], w_[:, kc, :],
                       start=(nmm == 0), stop=False, out=pS.ap[:, 0:8],
                       wait=[t_xn2ev[m]] if nmm == 0 else ())
                    nmm += 1
            for kc in range(8):
                mm(pS, xloT_st[:, kc, :], rw_hi[:, kc, :],
                   start=False, stop=(kc == 7), out=pS.ap[:, 0:8],
                   wait=[t_loev] if kc == 0 else (), inc=(kc == 7))
            lgt = lgt2[m % 2]
            sxv = sxv2[m % 2]
            prev_gates = t_gates[m - 2] if m >= 2 else None
            t_lg = ev("vector", lambda E_, lgt=lgt: E_.tensor_copy(
                lgt[:], pS.ap[:, 0:8]), bank=pS, wait=[prev_gates])
            if debug_taps:
                P.add("vector", lambda E_, m=m: E_.tensor_copy(
                    taps_lgt_sb[:, m, :], lgt[:]), wait=[t_lg], inc_by=1)
            V, G_ = "vector", "gpsimd"
            t1 = P.add(V, lambda E_, sxv=sxv, lgt=lgt: E_.reduce_max(sxv["tmp"][:, 0:1], lgt[:], axis=AX.X),
                       wait=[t_lg], inc_by=1)
            t5a = P.add(V, lambda E_, sxv=sxv, lgt=lgt: E_.tensor_scalar(
                sxv["d"][:], lgt[:], sxv["tmp"][:, 0:1], None, op0=ALU.subtract),
                wait=[t1, prev_gates], inc_by=1)
            t2 = P.add(G_, lambda E_, sxv=sxv, lgt=lgt: E_.tensor_scalar(
                sxv["ind1"][:], sxv["d"][:], 0.0, None, op0=ALU.is_ge),
                wait=[t5a], inc_by=1)
            t3 = P.add("scalar", lambda E_, sxv=sxv, lgt=lgt: E_.activation(
                sxv["absS"][:], lgt[:], AF.Abs),
                wait=[t_lg, prev_gates], inc_by=1)
            t4 = P.add(V, lambda E_, sxv=sxv, lgt=lgt: E_.tensor_scalar(
                sxv["f1"][:], sxv["absS"][:], sxv["tmp"][:, 0:1], None, op0=ALU.max),
                wait=[t3, t1], inc_by=1)
            t5 = t5a
            t6 = P.add(V, lambda E_, sxv=sxv, lgt=lgt: E_.scalar_tensor_tensor(
                sxv["t"][:], sxv["f1"][:], 0.02, sxv["d"][:], op0=ALU.mult, op1=ALU.add),
                wait=[t4, t5], inc_by=1)
            t7 = P.add(V, lambda E_, sxv=sxv, lgt=lgt: E_.tensor_scalar(
                sxv["u1"][:], sxv["t"][:], 0.0, None, op0=ALU.is_ge), wait=[t6], inc_by=1)
            t8 = P.add(V, lambda E_, sxv=sxv, lgt=lgt: E_.tensor_scalar(
                sxv["tmp"][:, 1:2], sxv["tmp"][:, 0:1], -1.0, None, op0=ALU.mult),
                wait=[t1], inc_by=1)
            t9 = P.add("scalar", lambda E_, sxv=sxv, lgt=lgt: E_.activation(
                sxv["ebuf"][:], lgt[:], AF.Exp, bias=sxv["tmp"][:, 1:2]),
                wait=[t8, t_lg, prev_gates], inc_by=1)
            t10 = P.add(G_, lambda E_, sxv=sxv, lgt=lgt: E_.tensor_tensor(
                sxv["eu"][:], sxv["ebuf"][:], sxv["u1"][:], op=ALU.mult),
                wait=[t9, t7], inc_by=1)
            t11 = P.add(V, lambda E_, sxv=sxv, lgt=lgt: E_.reduce_sum(
                sxv["tmp"][:, 2:3], sxv["eu"][:], axis=AX.X), wait=[t10], inc_by=1)
            t12 = P.add(V, lambda E_, sxv=sxv, lgt=lgt: E_.reciprocal(
                sxv["tmp"][:, 3:4], sxv["tmp"][:, 2:3]), wait=[t11], inc_by=1)
            t13 = P.add(V, lambda E_, sxv=sxv, lgt=lgt: E_.scalar_tensor_tensor(
                sxv["ms"][:], sxv["ind1"][:], -1e30, lgt[:], op0=ALU.mult, op1=ALU.add),
                wait=[t2, prev_gates], inc_by=1)
            t14 = P.add(V, lambda E_, sxv=sxv, lgt=lgt: E_.reduce_max(
                sxv["tmp"][:, 4:5], sxv["ms"][:], axis=AX.X), wait=[t13], inc_by=1)
            t15a = P.add(V, lambda E_, sxv=sxv, lgt=lgt: E_.tensor_scalar(
                sxv["d2"][:], sxv["ms"][:], sxv["tmp"][:, 4:5], None, op0=ALU.subtract),
                wait=[t14], inc_by=1)
            t15 = P.add(G_, lambda E_, sxv=sxv, lgt=lgt: E_.tensor_scalar(
                sxv["ind2"][:], sxv["d2"][:], 0.0, None, op0=ALU.is_ge),
                wait=[t15a], inc_by=1)
            t16 = P.add(V, lambda E_, sxv=sxv, lgt=lgt: E_.tensor_scalar(
                sxv["f1"][:], sxv["absS"][:], sxv["tmp"][:, 4:5], None, op0=ALU.max),
                wait=[t14, t6], inc_by=1)
            t17 = P.add(V, lambda E_, sxv=sxv, lgt=lgt: E_.tensor_scalar(
                sxv["d"][:], lgt[:], sxv["tmp"][:, 4:5], None, op0=ALU.subtract),
                wait=[t14, t6], inc_by=1)
            t18 = P.add(V, lambda E_, sxv=sxv, lgt=lgt: E_.scalar_tensor_tensor(
                sxv["t"][:], sxv["f1"][:], 0.02, sxv["d"][:], op0=ALU.mult, op1=ALU.add),
                wait=[t16, t17, t7], inc_by=1)
            t19 = P.add(V, lambda E_, sxv=sxv, lgt=lgt: E_.tensor_scalar(
                sxv["u1"][:], sxv["t"][:], 0.0, None, op0=ALU.is_ge),
                wait=[t18, t10], inc_by=1)
            t20 = P.add(V, lambda E_, sxv=sxv, lgt=lgt: E_.tensor_scalar(
                sxv["tmp"][:, 5:6], sxv["tmp"][:, 4:5], -1.0, None, op0=ALU.mult),
                wait=[t14, t9], inc_by=1)
            t21 = P.add("scalar", lambda E_, sxv=sxv, lgt=lgt: E_.activation(
                sxv["ebuf"][:], sxv["ms"][:], AF.Exp, bias=sxv["tmp"][:, 5:6]),
                wait=[t20, t10], inc_by=1)
            t22 = P.add(G_, lambda E_, sxv=sxv, lgt=lgt: E_.tensor_tensor(
                sxv["eu"][:], sxv["ebuf"][:], sxv["u1"][:], op=ALU.mult),
                wait=[t21, t19], inc_by=1)
            t23 = P.add(V, lambda E_, sxv=sxv, lgt=lgt: E_.reduce_sum(
                sxv["tmp"][:, 6:7], sxv["eu"][:], axis=AX.X), wait=[t22], inc_by=1)
            t24 = P.add(V, lambda E_, sxv=sxv, lgt=lgt: E_.reciprocal(
                sxv["tmp"][:, 7:8], sxv["tmp"][:, 6:7]), wait=[t23], inc_by=1)
            t25 = P.add(G_, lambda E_, sxv=sxv, lgt=lgt: E_.tensor_scalar(
                sxv["ebuf"][:], sxv["ind1"][:], sxv["tmp"][:, 3:4], None, op0=ALU.mult),
                wait=[t12, t21, t22], inc_by=1)
            t_gates[m] = P.add(V, lambda E_, m=m, sxv=sxv: E_.scalar_tensor_tensor(
                gcol[:, m, :], sxv["ind2"][:], sxv["tmp"][:, 7:8], sxv["ebuf"][:],
                op0=ALU.mult, op1=ALU.add), wait=[t24, t25, t15], inc_by=1)
        if debug_taps:
            dma("sync", "tap", taps["t_gates"], gcol.rearrange("p a b -> p (a b)"),
                wait=[t_gates[15]])
            dma("sync", "tap", taps["t_lgt"], taps_lgt_sb.rearrange("p a b -> p (a b)"),
                wait=[("vector", P.counts["vector"])])

        for m in range(16):
            phaseB_m(m)
        tok_fc_pe = [("tensor", P.counts["tensor"])]
        BAR_B2 = P.all_toks()

        moeS = dict(prev_gates_pe=None, prev_down_pe=None)
        prev_hev = {}
        t_y_dma = []
        t_oacc_prev = [None] * 8
        prev_yadd = [None]


        def moe_half(hf):
            for e_ in range(E):
                wld = [moeS["prev_gates_pe"]]
                if hf == 0 and e_ == 0:
                    t_wg, t_wu = pre_wg, pre_wu
                else:
                    t_wg = dma("gpsimd", "gw", wgt,
                               d_wgT[e_].rearrange("(c p) n -> p c n", p=128), wait=wld)
                    t_wu = dma("gpsimd", "gw", wut,
                               d_wuT[e_].rearrange("(c p) n -> p c n", p=128), wait=wld)
                wldd = [moeS["prev_down_pe"]]
                if hf == 0 and e_ == 0:
                    wldd += BAR_B2
                t_wd = dma("gpsimd", "gd", wdt,
                           d_wdT[e_].rearrange("(c p) n -> p c n", p=128),
                           wait=wldd)
                t_h_ev = None
                first_gate_w = [t_wu, tok_fc_pe[0]]
                for f in range(16):
                    for nn in range(2):
                        bg_ = pA if nn == 0 else pC
                        bu_ = pB if nn == 0 else pD
                        for kc in range(8):
                            mm(bg_, wgt[:, kc, f * 128:(f + 1) * 128],
                               xn2T[:, kc, hf * 1024 + nn * 512:hf * 1024 + (nn + 1) * 512],
                               start=(kc == 0), stop=(kc == 7),
                               wait=first_gate_w if (f == 0 and nn == 0 and kc == 0) else (),
                               inc=(kc == 7))
                        for kc in range(8):
                            mm(bu_, wut[:, kc, f * 128:(f + 1) * 128],
                               xn2T[:, kc, hf * 1024 + nn * 512:hf * 1024 + (nn + 1) * 512],
                               start=(kc == 0), stop=(kc == 7),
                               wait=[t_wu] if (f == 0 and nn == 0 and kc == 0) else (),
                               inc=(kc == 7))
                        t_si = ev("scalar", lambda E_, nn=nn, bg_=bg_: E_.activation(
                            silu_st[nn], bg_.ap, AF.Silu), bank=bg_,
                            wait=[prev_hev.get(nn), (t_bfc if (hf == 0 and e_ == 0) else None)])
                        t_h_ev = ev("vector", lambda E_, f=f, nn=nn, bu_=bu_:
                                    E_.tensor_tensor(
                                        hT[:, f, nn * 512:(nn + 1) * 512],
                                        silu_st[nn], bu_.ap, op=ALU.mult),
                                    bank=bu_, wait=[t_si])
                        prev_hev[nn] = t_h_ev
                    yield
                moeS["prev_gates_pe"] = ("tensor", P.counts["tensor"])
                for m8 in range(8):
                    mslot = hf * 8 + m8
                    for nn in range(2):
                        bank = pA if nn == 0 else pB
                        for kc in range(16):
                            mm(bank, hT[:, kc, m8 * 128:(m8 + 1) * 128],
                               wdt[:, kc, nn * 512:(nn + 1) * 512],
                               start=(kc == 0), stop=(kc == 15),
                               wait=[t_h_ev, t_wd] if (m8 == 0 and nn == 0 and kc == 0) else (),
                               inc=(kc == 15))
                        if e_ == 0:
                            ev("vector", lambda E_, m8=m8, nn=nn, bank=bank, mslot=mslot:
                               E_.tensor_scalar(
                                   out_acc[:, m8, nn * 512:(nn + 1) * 512], bank.ap,
                                   gcol[:, mslot, 0:1], None, op0=ALU.mult),
                               bank=bank,
                               wait=[t_gates[mslot], t_oacc_prev[m8]])
                        else:
                            ev("vector", lambda E_, m8=m8, nn=nn, bank=bank, mslot=mslot, e_=e_:
                               E_.scalar_tensor_tensor(
                                   out_acc[:, m8, nn * 512:(nn + 1) * 512], bank.ap,
                                   gcol[:, mslot, e_:e_ + 1],
                                   out_acc[:, m8, nn * 512:(nn + 1) * 512],
                                   op0=ALU.mult, op1=ALU.add), bank=bank)
                    if m8 % 2 == 1:
                        yield
                moeS["prev_down_pe"] = ("tensor", P.counts["tensor"])
            for m8 in range(8):
                mslot = hf * 8 + m8
                t_xr = dma("sync", "fin", xtF, i_xmid[128 * mslot:128 * (mslot + 1), :],
                           wait=[t_xmid_dma[mslot], prev_yadd[0],
                                 ("vector", P.counts["vector"]),
                                 (t_y_dma[-1] if t_y_dma else None)])
                t_add = P.add("vector", lambda E_, m8=m8: E_.tensor_tensor(
                    xtF, xtF, out_acc[:, m8, :], op=ALU.add),
                    wait=[t_xr], inc_by=1)
                t_oacc_prev[m8] = t_add
                prev_yadd[0] = t_add
                t_y_dma.append(dma("sync", "fin", d_y[128 * mslot:128 * (mslot + 1), :],
                                   xtF, wait=[t_add]))

        for _ in moe_half(0):
            pass
        for _ in moe_half(1):
            pass

        P.add("gpsimd", lambda E_: E_.engine_nop(), wait=P.all_toks(), inc_by=1)

        with nc.Block() as block:
            P.emit(block, sems)
    return nc


_eye = np.eye(128).astype(BFD)
CONSTS = dict(
    eye_bf=_eye,
    eye7_bf=(7.0 * np.eye(128)).astype(BFD),
    eye15_bf=(15.0 * np.eye(128)).astype(BFD),
    eye325_bf=(3.25 * np.eye(128)).astype(BFD),
    ones_col_bf=np.ones((128, 1), dtype=BFD),
    ones_row_bf=np.ones((1, 128), dtype=BFD),
    ones_row_f=np.ones((1, 128), dtype=np.float32),
    pairsel=np.stack([(np.arange(128) < 64).astype(np.float32),
                      (np.arange(128) >= 64).astype(np.float32)], axis=0),
)


def make_in_maps(L0_attn_x, L0_attn_w_qkv, L0_attn_w_conv, L0_attn_w_fc,
                 L0_attn_b_fc, L0_ffn_router_w, L0_ffn_w_gate, L0_ffn_w_up,
                 L0_ffn_w_down):
    x = np.ascontiguousarray(np.asarray(L0_attn_x, dtype=np.float32))
    wqkv = np.asarray(L0_attn_w_qkv, dtype=np.float32)
    wconv = np.asarray(L0_attn_w_conv, dtype=np.float32).reshape(16, KC)
    wfc0 = np.asarray(L0_attn_w_fc, dtype=np.float32)
    head_perm = []
    for pr in range(4):
        head_perm += [2 * pr, 2 * pr + 1, 8 + 2 * pr, 8 + 2 * pr + 1]
    row_perm = np.concatenate([np.arange(64) + 64 * h for h in head_perm])
    wfc = np.ascontiguousarray(wfc0[row_perm])
    bfc = np.asarray(L0_attn_b_fc, dtype=np.float32).reshape(1, D)
    rwT = np.ascontiguousarray(np.asarray(L0_ffn_router_w, dtype=np.float32).T)
    wgT = np.ascontiguousarray(np.asarray(L0_ffn_w_gate, dtype=np.float32).transpose(0, 2, 1))
    wuT = np.ascontiguousarray(np.asarray(L0_ffn_w_up, dtype=np.float32).transpose(0, 2, 1))
    wdT = np.ascontiguousarray(np.asarray(L0_ffn_w_down, dtype=np.float32).transpose(0, 2, 1))
    xf = x.reshape(4 * N, D)
    in_maps = []
    for c in range(NCORES):
        b, hg = c // 2, c % 2
        wq_sl = np.stack([wqkv[:, part * 1024 + hg * 512: part * 1024 + (hg + 1) * 512]
                          for part in range(3)], axis=0)
        wq_r = np.zeros((4, 128, 8, 384), dtype=np.float32)
        for pr in range(4):
            for blk in range(3):
                blkcols = wq_sl[blk][:, 128 * pr:128 * (pr + 1)]
                wq_r[pr, :, :, blk * 128:(blk + 1) * 128] = (
                    blkcols.reshape(8, 128, 128).transpose(1, 0, 2))
        in_maps.append(dict(
            xb=np.ascontiguousarray(x[b]),
            xown=np.ascontiguousarray(xf[TOK * c:TOK * (c + 1)]),
            wqkv=np.ascontiguousarray(wq_r),
            wfc=wfc, bfc=bfc,
            wconv=np.ascontiguousarray(wconv[hg * 8:(hg + 1) * 8]),
            rwT=rwT, wgT=wgT, wuT=wuT, wdT=wdT,
            **CONSTS,
        ))
    return in_maps


_NC_CACHE = {}


def kernel(**inputs):
    in_maps = make_in_maps(**inputs)
    if "nc" not in _NC_CACHE:
        _NC_CACHE["nc"] = build_nc(debug_taps=False)
    nc = _NC_CACHE["nc"]
    res = run_bass_kernel_spmd(nc, in_maps, core_ids=list(range(NCORES)))
    out = np.concatenate([np.asarray(r["y"]) for r in res.results], axis=0)
    return out.reshape(B_, N, D)
